# version 06
# baseline (speedup 1.0000x reference)
import math
import numpy as np
import ml_dtypes
from contextlib import ExitStack

import concourse.bass as bass
import concourse.mybir as mybir
from concourse.bass_utils import run_bass_kernel_spmd

F32 = mybir.dt.float32
BF16 = mybir.dt.bfloat16
I32 = mybir.dt.int32
AF = mybir.ActivationFunctionType
ALU = mybir.AluOpType
AX = mybir.AxisListType

RSCALE = 1.0 / math.sqrt(128.0)


def ts(i, n):
    return bass.ts(i, n)


class Cfg:
    def __init__(self, D=2048, S=4096, HC=8, J=5461, NH=16, stop_after=None,
                 b5_skip=(), debug_out=False):
        self.stop_after = stop_after
        self.b5_skip = set(b5_skip)
        self.debug_out = debug_out
        self.D, self.S, self.HC, self.J, self.NH = D, S, HC, J, NH
        self.E = 128
        self.ND = D // 128
        self.NMT = S // 512
        self.NCK = S // 128
        self.TOWN = S // 2
        self.NTC = self.TOWN // 128
        self.NTT = self.TOWN // 512
        self.NJ = (J + 127) // 128
        self.JLAST = J - (self.NJ - 1) * 128
        self.NDH = self.ND // 2
        self.NDSS = (self.NDH * 128) // 512
        assert D % 1024 == 0 and S % 512 == 0 and NH == 2 * HC


class Sems:
    def __init__(self, nc, es):
        self.nc, self.es, self.h, self.n = nc, es, {}, {}

    def get(self, name):
        if name not in self.h:
            self.h[name] = self.es.enter_context(self.nc.semaphore(name))
            self.n[name] = 0
        return self.h[name]

    def tag(self, instr, name, k=1):
        instr.then_inc(self.get(name), k)
        self.n[name] += k
        return self.n[name]

    def mark(self, eng, name, k=1):
        return self.tag(eng.drain(), name, k)

    def val(self, name):
        self.get(name)
        return self.n[name]

    def wait(self, eng, name, v):
        if v > 0:
            eng.wait_ge(self.get(name), v)


def batches(n, size=8):
    out, i = [], 0
    while i < n:
        out.append(list(range(i, min(i + size, n))))
        i += size
    return out


def build(cfg: Cfg):
    nc = bass.Bass("TRN2", target_bir_lowering=False, detect_race_conditions=False)
    D, S, HC, E, ND = cfg.D, cfg.S, cfg.HC, cfg.E, cfg.ND
    NMT, NCK, TOWN, NTC, NTT = cfg.NMT, cfg.NCK, cfg.TOWN, cfg.NTC, cfg.NTT
    J, NJ, JLAST, NH = cfg.J, cfg.NJ, cfg.JLAST, cfg.NH
    NDH, NDSS = cfg.NDH, cfg.NDSS
    NW = float(J * D)
    NDS = D // 512

    xk = nc.dram_tensor("xk", [S, D], F32, kind="ExternalInput").ap()
    xo = nc.dram_tensor("xo", [TOWN, D], F32, kind="ExternalInput").ap()
    wq = nc.dram_tensor("wq", [HC, E, D], F32, kind="ExternalInput").ap()
    wk = nc.dram_tensor("wk", [HC, E, D], F32, kind="ExternalInput").ap()
    wv = nc.dram_tensor("wv", [HC, E, D], F32, kind="ExternalInput").ap()
    wo = nc.dram_tensor("wo", [NH, D, E], F32, kind="ExternalInput").ap()
    w1 = nc.dram_tensor("w1", [J, D], F32, kind="ExternalInput").ap()
    w3 = nc.dram_tensor("w3", [J, D], F32, kind="ExternalInput").ap()
    w2 = nc.dram_tensor("w2", [D, J], F32, kind="ExternalInput").ap()
    roff = nc.dram_tensor("roff", [1, NTC], I32, kind="ExternalInput").ap()
    out = nc.dram_tensor("out", [TOWN, D], F32, kind="ExternalOutput").ap()

    dk = "ExternalOutput" if cfg.debug_out else "Internal"
    xn_dbg = (nc.dram_tensor("xn_dbg", [NCK + TOWN // 128, 128, D], BF16,
                             kind="ExternalOutput").ap() if cfg.debug_out else None)
    ln_dbg = (nc.dram_tensor("ln_dbg", [NCK + TOWN // 128, 128, 18], F32,
                             kind="ExternalOutput").ap() if cfg.debug_out else None)
    xn_own_sc = nc.dram_tensor("xn_own_sc", [ND, 128, TOWN], BF16, kind=dk).ap()
    k_sc = nc.dram_tensor("k_sc", [HC, E, S], BF16, kind=dk).ap()
    q_sc = nc.dram_tensor("q_sc", [HC, E, S], BF16, kind=dk).ap()
    v_sc = nc.dram_tensor("v_sc", [HC, 128, S], BF16, kind=dk).ap()
    ag_in = nc.dram_tensor("ag_in", [HC, E, S], BF16, kind="Internal").ap()
    ag_out = nc.dram_tensor("ag_out", [NH, E, S], BF16, kind="Internal").ap()
    g_sc = nc.dram_tensor("g_sc", [NJ, 128, TOWN], BF16, kind=dk).ap()
    r2_sc = nc.dram_tensor("r2_sc", [ND, 128, NJ * 128], BF16, kind=dk).ap()
    attn_sc = nc.dram_tensor("attn_sc", [TOWN, D], F32, kind=dk).ap()

    es = ExitStack()
    sm = Sems(nc, es)

    ident = nc.alloc_sbuf_tensor("ident", [128, 128], BF16).ap()
    maskst = nc.alloc_sbuf_tensor("maskst", [128, 128], F32).ap()
    ones_f32 = nc.alloc_sbuf_tensor("ones_f32", [128, 128], F32).ap()
    soff = nc.alloc_sbuf_tensor("soff", [1, NTC], I32).ap()

    SCH = 128 * 1024
    NWI = J * D
    NST1 = (NWI + SCH - 1) // SCH
    stats_plan = ([("w1", j) for j in range(NST1)] + [("w3", j) for j in range(NST1)]
                  + [("w2", j) for j in range(NST1)])
    NSTAT = len(stats_plan)
    s1cols = nc.alloc_sbuf_tensor("s1cols", [128, NST1], F32).ap()
    s3cols = nc.alloc_sbuf_tensor("s3cols", [128, NST1], F32).ap()
    s2cols = nc.alloc_sbuf_tensor("s2cols", [128, NST1], F32).ap()
    h1c = nc.alloc_sbuf_tensor("h1c", [128, 1], F32).ap()
    h3c = nc.alloc_sbuf_tensor("h3c", [128, 1], F32).ap()
    h2c = nc.alloc_sbuf_tensor("h2c", [128, 1], F32).ap()
    h1n = nc.alloc_sbuf_tensor("h1n", [128, 1], F32).ap()
    h3n = nc.alloc_sbuf_tensor("h3n", [128, 1], F32).ap()
    h2n = nc.alloc_sbuf_tensor("h2n", [128, 1], F32).ap()
    redc = nc.alloc_sbuf_tensor("redc", [128, 3], F32).ap()
    epsc = nc.alloc_sbuf_tensor("epsc", [128, 1], F32).ap()

    def stats_src(idx):
        kind, j = stats_plan[idx]
        t = {"w1": w1, "w3": w3, "w2": w2}[kind]
        flat = t.rearrange("a b -> (a b)")
        lo = j * SCH
        n = min(SCH, NWI - lo)
        p = n // 1024
        rem = n - p * 1024
        main = flat[lo: lo + p * 1024].rearrange("(p c) -> p c", p=p) if p else None
        tail = (flat[lo + p * 1024: lo + n].rearrange("(p c) -> p c", p=1)
                if rem else None)
        return main, tail, p

    def stats_col(idx):
        kind, j = stats_plan[idx]
        return {"w1": s1cols, "w3": s3cols, "w2": s2cols}[kind][:, j:j + 1]

    with nc.named_scope("b0"), nc.Block("b0") as blk:
        @blk.gpsimd
        def _(g):
            g.dma_start(out=soff[:], in_=roff).then_inc(sm.get("gld0"), 16)
            sm.tag(g.memset(ident[:], 1.0), "g0")
            g.wait_ge(sm.get("g0"), 1)
            sm.tag(g.affine_select(out=ident[:], in_=ident[:], compare_op=ALU.is_ge,
                                   fill=0.0, base=0, pattern=[[1, 128]],
                                   channel_multiplier=-1), "g0")
            g.wait_ge(sm.get("g0"), 2)
            sm.tag(g.affine_select(out=ident[:], in_=ident[:], compare_op=ALU.is_ge,
                                   fill=0.0, base=0, pattern=[[-1, 128]],
                                   channel_multiplier=1), "g0")
            sm.tag(g.memset(maskst[:], 0.0), "g0")
            g.wait_ge(sm.get("g0"), 4)
            g.affine_select(out=maskst[:], in_=maskst[:], compare_op=ALU.is_ge,
                            fill=-1e9, base=0, pattern=[[1, 128]],
                            channel_multiplier=-1)
            g.wait_ge(sm.get("gld0"), 16)

        @blk.vector
        def _(v):
            v.memset(ones_f32[:], 1.0)
            v.memset(epsc[:], 1e-5)
            v.memset(s1cols[:], 0.0)
            v.memset(s3cols[:], 0.0)
            v.memset(s2cols[:], 0.0)

    with (
        nc.sbuf_tensor("xn_fm", [128, ND * S], BF16) as xn_fm_g,
        nc.sbuf_tensor("statw", [128, 2 * 1024], F32) as statw_g,
    ):
        xn_fm = xn_fm_g[:]
        statw = statw_g[:]
        stld_at = []
        _cnt = {0: 0, 1: 0}
        for _sdi in range(NSTAT):
            _m, _t, _p = stats_src(_sdi)
            _cnt[_sdi % 2] += 16 * ((1 if _m is not None else 0)
                                    + (1 if _t is not None else 0))
            stld_at.append(("stld%d" % (_sdi % 2), _cnt[_sdi % 2]))

        def stats_dma(g, sdi):
            if sdi >= 2:
                sm.wait(g, "stdone", sdi - 1)
            main, tail, p = stats_src(sdi)
            sem = "stld%d" % (sdi % 2)
            if main is not None:
                sm.tag(g.dma_start(out=statw[0:p, ts(sdi % 2, 1024)], in_=main),
                       sem, 16)
            if tail is not None:
                sm.tag(g.dma_start(out=statw[p:p + 1, (sdi % 2) * 1024:
                                             (sdi % 2) * 1024 + tail.shape[1]],
                                   in_=tail), sem, 16)

        def stats_unit(v, sui):
            sm.wait(v, stld_at[sui][0], stld_at[sui][1])
            main, tail, p = stats_src(sui)
            rows = p + (1 if tail is not None else 0)
            sm.tag(v.tensor_reduce(out=stats_col(sui)[0:rows, :],
                                   in_=statw[0:rows, ts(sui % 2, 1024)],
                                   axis=AX.X, op=ALU.add,
                                   apply_absolute_value=True), "stdone")

        nchunks = NCK + NTC
        with (
            nc.sbuf_tensor("x_sb", [128, 3 * D], F32) as x_sb_g,
            nc.sbuf_tensor("trash", [128, D], BF16) as trash_g,
            nc.sbuf_tensor("xn_tm", [128, 2 * D], BF16) as xn_tm_g,
            nc.sbuf_tensor("obuf", [128, 2 * D], BF16) as obuf_g,
            nc.sbuf_tensor("lncol", [128, 18], F32) as lncol_g,
            nc.psum_tensor("pt0", [128, 1024], BF16) as pt0_g,
            nc.psum_tensor("pt1", [128, 1024], BF16) as pt1_g,
            nc.named_scope("b1a"), nc.Block("b1a") as blk,
        ):
            x_sb, trash = x_sb_g[:], trash_g[:]
            xn_tm, obuf, lncol = xn_tm_g[:], obuf_g[:], lncol_g[:]
            ptb = [pt0_g[:], pt1_g[:]]

            @blk.sync
            def _(sync):
                for i in range(nchunks):
                    if cfg.debug_out and i >= 1:
                        sm.wait(sync, "snm", i)
                        sm.tag(sync.dma_start(out=xn_dbg[i - 1],
                                              in_=xn_tm[:, ts((i - 1) % 2, D)]),
                               "xdbg", 16)
                        sm.tag(sync.dma_start(out=ln_dbg[i - 1], in_=lncol[:]),
                               "xdbg", 16)
                        sm.wait(sync, "xdbg", sm.val("xdbg"))
                    if i >= 3:
                        sm.wait(sync, "vself", i - 2)
                        sm.wait(sync, "sself", i - 2)
                    src = (xk[ts(i, 128), :] if i < NCK else xo[ts(i - NCK, 128), :])
                    sm.tag(sync.dma_start(out=x_sb[:, ts(i % 3, D)], in_=src),
                           "xld%d" % (i % 3), 16)
                    if i >= NCK + 1:
                        oj = i - NCK - 1
                        sm.wait(sync, "tev", (NCK + oj + 1) * ((ND + 7) // 8))
                        sm.tag(sync.dma_start(
                            out=xn_own_sc[:, :, ts(oj, 128)].rearrange(
                                "c p t -> p c t"),
                            in_=obuf[:, ts(oj % 2, D)].rearrange(
                                "p (c e) -> p c e", c=ND)), "ost%d" % (oj % 2), 16)
                if cfg.debug_out:
                    sm.wait(sync, "snm", nchunks)
                    sm.tag(sync.dma_start(out=xn_dbg[nchunks - 1],
                                          in_=xn_tm[:, ts((nchunks - 1) % 2, D)]),
                           "xdbg", 16)
                    sm.tag(sync.dma_start(out=ln_dbg[nchunks - 1], in_=lncol[:]),
                           "xdbg", 16)
                    sm.wait(sync, "xdbg", sm.val("xdbg"))
                oj = NTC - 1
                sm.wait(sync, "tev", (NCK + oj + 1) * ((ND + 7) // 8))
                sm.tag(sync.dma_start(
                    out=xn_own_sc[:, :, ts(oj, 128)].rearrange("c p t -> p c t"),
                    in_=obuf[:, ts(oj % 2, D)].rearrange("p (c e) -> p c e", c=ND)),
                    "ost%d" % (oj % 2), 16)
                sync.wait_ge(sm.get("ost0"), sm.val("ost0"))
                sync.wait_ge(sm.get("ost1"), sm.val("ost1"))

            TBAT = batches(ND, 8)
            NTB = len(TBAT)

            def evac(v, ci):
                if ci >= NCK:
                    oi = ci - NCK
                    if oi >= 2:
                        sm.wait(v, "ost%d" % (oi % 2), 16 * ((oi - 2) // 2 + 1))
                for bi, bat in enumerate(TBAT):
                    gidx = ci * NTB + bi
                    sm.wait(v, "ttr", gidx + 1)
                    pb = ptb[gidx % 2]
                    nb = len(bat)
                    if ci < NCK:
                        dst = xn_fm.rearrange("p (c s) -> p c s", c=ND)[
                            :, bat[0]:bat[0] + nb, ts(ci, 128)]
                    else:
                        oi = ci - NCK
                        dst = obuf[:, ts(oi % 2, D)].rearrange(
                            "p (c e) -> p c e", c=ND)[:, bat[0]:bat[0] + nb, :]
                    sm.tag(v.tensor_copy(out=dst,
                                         in_=pb[:, 0:nb * 128].rearrange(
                                             "p (c e) -> p c e", c=nb)), "tev")

            @blk.vector
            def _(v):
                sui = 0
                for i in range(nchunks):
                    sm.wait(v, "vself", i)
                    sl = i % 3
                    sm.wait(v, "xld%d" % sl, 16 * (i // 3 + 1))
                    c_s = lncol[:, sl:sl + 1]
                    c_m = lncol[:, 3 + sl:4 + sl]
                    c_v = lncol[:, 6 + sl:7 + sl]
                    c_r = lncol[:, 9 + sl:10 + sl]
                    c_b = lncol[:, 12 + sl:13 + sl]
                    c_q = lncol[:, 15 + sl:16 + sl]
                    c1 = sm.tag(v.reduce_sum(out=c_s, in_=x_sb[:, ts(sl, D)],
                                             axis=AX.X), "vrd")
                    sm.wait(v, "vrd", c1)
                    c2 = sm.tag(v.tensor_scalar(out=c_m, in0=c_s, scalar1=1.0 / D,
                                                scalar2=None, op0=ALU.mult), "vrd")
                    sm.wait(v, "ssq", i + 1)
                    sm.wait(v, "vrd", c2)
                    c3 = sm.tag(v.tensor_mul(out=c_v, in0=c_m, in1=c_m), "vrd")
                    sm.wait(v, "vrd", c3)
                    sm.tag(v.scalar_tensor_tensor(out=c_v, in0=c_q, scalar=1.0 / D,
                                                  in1=c_v, op0=ALU.mult,
                                                  op1=ALU.subtract), "vvar")
                    sm.wait(v, "srs", i + 1)
                    sm.tag(v.scalar_tensor_tensor(out=c_b, in0=c_m, scalar=-1.0,
                                                  in1=c_r, op0=ALU.mult,
                                                  op1=ALU.mult), "vnb")
                    if i >= 1:
                        evac(v, i - 1)
                    if sui < NSTAT and i % 2 == 0:
                        stats_unit(v, sui)
                        sui += 1
                    sm.mark(v, "vself")
                evac(v, nchunks - 1)
                sm.n["sui_done"] = sui

            @blk.scalar
            def _(s):
                for i in range(nchunks):
                    sm.wait(s, "sself", i)
                    sl = i % 3
                    sm.wait(s, "xld%d" % sl, 16 * (i // 3 + 1))
                    c_v = lncol[:, 6 + sl:7 + sl]
                    c_r = lncol[:, 9 + sl:10 + sl]
                    c_q = lncol[:, 15 + sl:16 + sl]
                    sm.tag(s.activation(out=trash[:], in_=x_sb[:, ts(sl, D)],
                                        func=AF.Square, accum_out=c_q), "ssq")
                    sm.wait(s, "vvar", i + 1)
                    s.activation(out=c_r, in_=c_v, func=AF.Ln, bias=epsc)
                    sm.tag(s.activation(out=c_r, in_=c_r, func=AF.Exp, scale=-0.5),
                           "srs")
                    sm.wait(s, "vnb", i + 1)
                    if i >= 2:
                        sm.wait(s, "ttr", (i - 1) * ((ND + 7) // 8))
                    sm.tag(s.activation(out=xn_tm[:, ts(i % 2, D)],
                                        in_=x_sb[:, ts(sl, D)], func=AF.Identity,
                                        bias=lncol[:, 12 + sl:13 + sl], scale=c_r),
                           "snm")
                    sm.mark(s, "sself")

            @blk.tensor
            def _(t):
                for i in range(nchunks):
                    sm.wait(t, "snm", i + 1)
                    for bi, bat in enumerate(TBAT):
                        gidx = i * NTB + bi
                        if gidx >= 2:
                            sm.wait(t, "tev", gidx - 1)
                        pb = ptb[gidx % 2]
                        last = None
                        for k, c in enumerate(bat):
                            last = t.transpose(pb[:, ts(k, 128)],
                                               xn_tm[:, (i % 2) * D + c * 128:
                                                     (i % 2) * D + (c + 1) * 128],
                                               ident)
                        sm.tag(last, "ttr")

            @blk.gpsimd
            def _(g):
                nb1a = min((nchunks + 1) // 2, NSTAT)
                for sdi in range(min(nb1a + 2, NSTAT)):
                    stats_dma(g, sdi)
                sm.n["sdi_done"] = min(nb1a + 2, NSTAT)

        if cfg.stop_after == "b1a":
            return nc, es
        with (
            nc.sbuf_tensor("wnat", [128, 2 * 3 * D], BF16) as wnat_g,
            nc.sbuf_tensor("wT", [128, 3 * D], BF16) as wT_g,
            nc.sbuf_tensor("kqb", [128, 4 * 512], BF16) as kqb_g,
            nc.sbuf_tensor("vb", [128, 4 * 512], BF16) as vb_g,
            nc.psum_tensor("pw0", [128, 1024], BF16) as pw0_g,
            nc.psum_tensor("pw1", [128, 1024], BF16) as pw1_g,
            nc.psum_tensor("kp0", [128, 512], F32) as kp0_g,
            nc.psum_tensor("kp1", [128, 512], F32) as kp1_g,
            nc.psum_tensor("vp0", [128, 512], F32) as vp0_g,
            nc.psum_tensor("vp1", [128, 512], F32) as vp1_g,
            nc.named_scope("b1b"), nc.Block("b1b") as blk,
        ):
            wnat, wT, kqb, vb = wnat_g[:], wT_g[:], kqb_g[:], vb_g[:]
            pwb = [pw0_g[:], pw1_g[:]]
            kpb = [kp0_g[:], kp1_g[:]]
            vpb = [vp0_g[:], vp1_g[:]]
            WBAT = batches(3 * ND, 8)
            NWB = len(WBAT)
            NVG = NCK // 4

            @blk.gpsimd
            def _(g):
                sdi = sm.n.get("sdi_done", NSTAT)
                for h in range(HC):
                    if h >= 2:
                        sm.wait(g, "twtr", (h - 1) * NWB)
                    for mi, wsrc in enumerate((wq, wk, wv)):
                        sm.tag(g.dma_start(
                            out=wnat[:, (h % 2) * 3 * D + mi * D:
                                     (h % 2) * 3 * D + (mi + 1) * D],
                            in_=wsrc[h]), "gwl%d" % (h % 2), 16)
                    quota = (NSTAT - sm.n.get("sdi_done", NSTAT) + HC - 1) // HC
                    for _ in range(quota):
                        if sdi < NSTAT:
                            stats_dma(g, sdi)
                            sdi += 1
                while sdi < NSTAT:
                    stats_dma(g, sdi)
                    sdi += 1

            @blk.tensor
            def _(t):
                for h in range(HC):
                    sm.wait(t, "gwl%d" % (h % 2), 16 * 3 * (h // 2 + 1))
                    base = (h % 2) * 3 * D
                    for bi, bat in enumerate(WBAT):
                        gidx = h * NWB + bi
                        if gidx >= 2:
                            sm.wait(t, "twev", gidx - 1)
                        pb = pwb[gidx % 2]
                        last = None
                        for k, c in enumerate(bat):
                            last = t.transpose(pb[:, ts(k, 128)],
                                               wnat[:, base + c * 128:
                                                    base + (c + 1) * 128], ident)
                        sm.tag(last, "twtr")
                    sm.wait(t, "twev", (h + 1) * NWB)
                    for mi in range(2):
                        woff = (1 - mi) * D
                        for mt in range(NMT):
                            idx = h * 2 * NMT + mi * NMT + mt
                            if idx >= 2:
                                sm.wait(t, "skv", idx - 1)
                            pb = kpb[idx % 2]
                            last = None
                            for c in range(ND):
                                last = t.matmul(pb,
                                                wT[:, woff + c * 128:
                                                   woff + (c + 1) * 128],
                                                xn_fm[:, c * S + mt * 512:
                                                      c * S + mt * 512 + 512],
                                                start=(c == 0), stop=(c == ND - 1))
                            sm.tag(last, "tkq")
                    for vg in range(NVG):
                        idx = h * NVG + vg
                        if idx >= 2:
                            sm.wait(t, "svv", idx - 1)
                        pb = vpb[idx % 2]
                        last = None
                        for u in range(4):
                            msub = vg * 4 + u
                            for c in range(ND):
                                last = t.matmul(pb[:, ts(u, 128)],
                                                xn_fm[:, c * S + msub * 128:
                                                      c * S + (msub + 1) * 128],
                                                wT[:, 2 * D + c * 128:
                                                   2 * D + (c + 1) * 128],
                                                start=(c == 0), stop=(c == ND - 1))
                        sm.tag(last, "tvv")

            @blk.scalar
            def _(s):
                for h in range(HC):
                    sm.wait(s, "sbself", h)
                    if h >= 1:
                        sm.wait(s, "tvv", h * NVG)
                    for bi, bat in enumerate(WBAT):
                        gidx = h * NWB + bi
                        sm.wait(s, "twtr", gidx + 1)
                        ncols = len(bat) * 128
                        sm.tag(s.copy(out=wT[:, bat[0] * 128: bat[0] * 128 + ncols],
                                      in_=pwb[gidx % 2][:, 0:ncols]), "twev")
                    for mi in range(2):
                        for mt in range(NMT):
                            idx = h * 2 * NMT + mi * NMT + mt
                            sm.wait(s, "tkq", idx + 1)
                            if idx >= 4:
                                sm.wait(s, "kqst%d" % (idx % 4),
                                        16 * ((idx - 4) // 4 + 1))
                            slot = kqb[:, ts(idx % 4, 512)]
                            if mi == 1:
                                ins = s.activation(out=slot, in_=kpb[idx % 2],
                                                   func=AF.Copy, scale=RSCALE)
                            else:
                                ins = s.copy(out=slot, in_=kpb[idx % 2])
                            sm.tag(ins, "skv")
                    for vg in range(NVG):
                        idx = h * NVG + vg
                        sm.wait(s, "tvv", idx + 1)
                        if idx >= 4:
                            sm.wait(s, "vst%d" % (idx % 4),
                                    16 * ((idx - 4) // 4 + 1))
                        sm.tag(s.copy(out=vb[:, ts(idx % 4, 512)],
                                      in_=vpb[idx % 2]), "svv")
                    sm.mark(s, "sbself")

            @blk.sync
            def _(sync):
                for h in range(HC):
                    for mi in range(2):
                        dsc = (k_sc, q_sc)[mi]
                        for mt in range(NMT):
                            idx = h * 2 * NMT + mi * NMT + mt
                            sm.wait(sync, "skv", idx + 1)
                            sm.tag(sync.dma_start(out=dsc[h][:, ts(mt, 512)],
                                                  in_=kqb[:, ts(idx % 4, 512)]),
                                   "kqst%d" % (idx % 4), 16)
                    for vg in range(NVG):
                        idx = h * NVG + vg
                        sm.wait(sync, "svv", idx + 1)
                        sm.tag(sync.dma_start(out=v_sc[h][:, ts(vg, 512)],
                                              in_=vb[:, ts(idx % 4, 512)]),
                               "vst%d" % (idx % 4), 16)
                for q in range(4):
                    sync.wait_ge(sm.get("kqst%d" % q), sm.val("kqst%d" % q))
                    sync.wait_ge(sm.get("vst%d" % q), sm.val("vst%d" % q))

            @blk.vector
            def _(v):
                sui = sm.n.get("sui_done", 0)
                k = 0
                while sui < NSTAT:
                    sm.wait(v, "v1bself", k)
                    stats_unit(v, sui)
                    sm.mark(v, "v1bself")
                    sui += 1
                    k += 1

    if cfg.stop_after == "b1b":
        return nc, es
    with (
        nc.psum_tensor("psb", [128, 3], F32) as psb_g,
        nc.named_scope("b1c"), nc.Block("b1c") as blk,
    ):
        psb = psb_g[:]

        @blk.vector
        def _(v):
            sm.tag(v.tensor_reduce(out=redc[:, 0:1], in_=s1cols[:, 0:NST1],
                                   axis=AX.X, op=ALU.add), "vqred")
            sm.tag(v.tensor_reduce(out=redc[:, 1:2], in_=s3cols[:, 0:NST1],
                                   axis=AX.X, op=ALU.add), "vqred")
            sm.tag(v.tensor_reduce(out=redc[:, 2:3], in_=s2cols[:, 0:NST1],
                                   axis=AX.X, op=ALU.add), "vqred")

        @blk.tensor
        def _(t):
            sm.wait(t, "vqred", 3)
            sm.tag(t.matmul(psb, ones_f32, redc[:, 0:3], start=True, stop=True),
                   "tred")

        @blk.scalar
        def _(s):
            sm.wait(s, "tred", 1)
            s.mul(out=h1c, in_=psb[:, 0:1], mul=0.5 / NW)
            s.mul(out=h3c, in_=psb[:, 1:2], mul=0.5 / NW)
            s.mul(out=h2c, in_=psb[:, 2:3], mul=0.5 / NW)
            s.mul(out=h1n, in_=psb[:, 0:1], mul=-0.5 / NW)
            s.mul(out=h3n, in_=psb[:, 1:2], mul=-0.5 / NW)
            s.mul(out=h2n, in_=psb[:, 2:3], mul=-0.5 / NW)

    if cfg.stop_after == "b1c":
        return nc, es
    pair_list = []
    for h in range(HC):
        for ti in range(NMT):
            nmc = 4 * ti + 4
            for pr in range(nmc // 2):
                pair_list.append((h, ti, pr, nmc))
    diag_before = []
    dcnt = 0
    for (h, ti, pr, nmc) in pair_list:
        diag_before.append(dcnt)
        if 2 * pr + 1 >= nmc - 4:
            dcnt += 1
    paircum = []
    acc = 0
    for ti in range(NMT):
        acc += (4 * ti + 4) // 2
        paircum.append(acc)

    def pair_cum(h, ti):
        return h * paircum[-1] + paircum[ti]

    with (
        nc.sbuf_tensor("ksb", [128, 2 * S], BF16) as ksb_g,
        nc.sbuf_tensor("qsb", [128, 2 * S], BF16) as qsb_g,
        nc.sbuf_tensor("vaug", [128, 2 * NCK * 129], BF16) as vaug_g,
        nc.sbuf_tensor("p_sb", [128, 2 * 1024], BF16) as p_sb_g,
        nc.sbuf_tensor("ztm", [128, 2 * S], BF16) as ztm_g,
        nc.sbuf_tensor("zfm", [128, 2 * S], BF16, is_transpose_dest=True) as zfm_g,
        nc.sbuf_tensor("recipb", [128, 8], F32) as recip_g,
        nc.psum_tensor("sp0", [128, 1024], F32) as sp0_g,
        nc.psum_tensor("sp1", [128, 1024], F32) as sp1_g,
        nc.psum_tensor("zp0", [128, 512], F32) as zp0_g,
        nc.psum_tensor("zp1", [128, 512], F32) as zp1_g,
        nc.psum_tensor("zp2", [128, 512], F32) as zp2_g,
        nc.psum_tensor("zp3", [128, 512], F32) as zp3_g,
        nc.named_scope("b2"), nc.Block("b2") as blk,
    ):
        ksb, qsb, vaug = ksb_g[:], qsb_g[:], vaug_g[:]
        p_sb, ztm, zfm, recipb = p_sb_g[:], ztm_g[:], zfm_g[:], recip_g[:]
        spb = [sp0_g[:], sp1_g[:]]
        zpb = [zp0_g[:], zp1_g[:], zp2_g[:], zp3_g[:]]

        def vaug_view(slot):
            return vaug[:, slot * NCK * 129:(slot + 1) * NCK * 129].rearrange(
                "p (m r) -> p m r", r=129)

        @blk.sync
        def _(sync):
            for h in range(HC + 1):
                if h < HC:
                    if h >= 2:
                        sm.wait(sync, "pav", pair_cum(h - 2, NMT - 1))
                    sm.tag(sync.dma_start(out=ksb[:, ts(h % 2, S)], in_=k_sc[h]),
                           "akl%d" % (h % 2), 16)
                    sm.tag(sync.dma_start(out=qsb[:, ts(h % 2, S)], in_=q_sc[h]),
                           "akl%d" % (h % 2), 16)
                    sm.tag(sync.dma_start(out=vaug_view(h % 2)[:, :, 0:128],
                                          in_=v_sc[h].rearrange(
                                              "p (m c) -> p m c", c=128)),
                           "akl%d" % (h % 2), 16)
                if h >= 1:
                    hz = h - 1
                    sm.wait(sync, "vdiv", (hz + 1) * NMT * 4)
                    for ci in range(NCK):
                        sm.tag(sync.dma_start_transpose(
                            out=zfm[:, (hz % 2) * S + ci * 128:
                                    (hz % 2) * S + (ci + 1) * 128],
                            in_=ztm[:, (hz % 2) * S + ci * 128:
                                    (hz % 2) * S + (ci + 1) * 128]), "ztr", 16)
                    sm.wait(sync, "ztr", sm.val("ztr"))
                    sm.tag(sync.dma_start(out=ag_in[hz],
                                          in_=zfm[:, ts(hz % 2, S)]),
                           "zst%d" % (hz % 2), 16)
            sync.wait_ge(sm.get("zst0"), sm.val("zst0"))
            sync.wait_ge(sm.get("zst1"), sm.val("zst1"))

        @blk.vector
        def _(v):
            v.memset(vaug_view(0)[:, :, 128:129], 1.0)
            sm.tag(v.memset(vaug_view(1)[:, :, 128:129], 1.0), "vone")
            for pidx, (h, ti, pr, nmc) in enumerate(pair_list):
                diag = (2 * pr + 1 >= nmc - 4)
                if diag:
                    sm.wait(v, "tsc", pidx + 1)
                    last = None
                    for u in range(2):
                        mc = 2 * pr + u
                        dd = mc - (nmc - 4)
                        if dd >= 0:
                            dlt = dd * 128
                            last = v.tensor_tensor(
                                out=spb[pidx % 2][:, u * 512 + dlt:
                                                  u * 512 + dlt + 128],
                                in0=spb[pidx % 2][:, u * 512 + dlt:
                                                  u * 512 + dlt + 128],
                                in1=maskst, op=ALU.add)
                    sm.tag(last, "vmk")
                if pr == nmc // 2 - 1:
                    tidx = h * NMT + ti
                    h2 = h % 2
                    sm.wait(v, "pav", pair_cum(h, ti))
                    if h >= 2:
                        sm.wait(v, "zst%d" % (h % 2), 16 * ((h - 2) // 2 + 1))
                    for tc in range(4):
                        sums = bass.AP(zpb[tc].tensor, zpb[tc].offset + 128,
                                       [list(zpb[tc].ap[0]), [1, 1]])
                        sm.tag(v.reciprocal(out=recipb[:, h2 * 4 + tc:
                                                       h2 * 4 + tc + 1],
                                            in_=sums), "vrc")
                    sm.wait(v, "vrc", sm.val("vrc"))
                    for tc in range(4):
                        ci = ti * 4 + tc
                        sm.tag(v.tensor_scalar(
                            out=ztm[:, h2 * S + ci * 128: h2 * S + (ci + 1) * 128],
                            in0=zpb[tc][:, 0:128],
                            scalar1=recipb[:, h2 * 4 + tc: h2 * 4 + tc + 1],
                            scalar2=None, op0=ALU.mult), "vdiv")

        @blk.scalar
        def _(s):
            for pidx, (h, ti, pr, nmc) in enumerate(pair_list):
                diag = (2 * pr + 1 >= nmc - 4)
                if diag:
                    sm.wait(s, "vmk", diag_before[pidx] + 1)
                else:
                    sm.wait(s, "tsc", pidx + 1)
                if pidx >= 2:
                    sm.wait(s, "pav", pidx - 1)
                full0 = (2 * pr) < (nmc - 4)
                full1 = (2 * pr + 1) < (nmc - 4)
                if full0 and full1:
                    sm.tag(s.activation(out=p_sb[:, ts(pidx % 2, 1024)],
                                        in_=spb[pidx % 2][:, 0:1024],
                                        func=AF.Exp), "sexp")
                else:
                    last = None
                    for u in range(2):
                        mc = 2 * pr + u
                        dd = mc - (nmc - 4)
                        dlt = max(dd, 0) * 128
                        last = s.activation(
                            out=p_sb[:, (pidx % 2) * 1024 + u * 512 + dlt:
                                     (pidx % 2) * 1024 + (u + 1) * 512],
                            in_=spb[pidx % 2][:, u * 512 + dlt:(u + 1) * 512],
                            func=AF.Exp)
                    sm.tag(last, "sexp")

        @blk.tensor
        def _(t):
            def av(pidx):
                h, ti, pr, nmc = pair_list[pidx]
                h2 = h % 2
                if pr == 0:
                    tidx = h * NMT + ti
                    if tidx >= 1:
                        sm.wait(t, "vdiv", tidx * 4)
                sm.wait(t, "sexp", pidx + 1)
                last = None
                for u in range(2):
                    mc = 2 * pr + u
                    dd = mc - (nmc - 4)
                    dlt = max(dd, 0) * 128
                    for tc in range(dlt // 128, 4):
                        last = t.matmul(
                            zpb[tc][:, 0:129],
                            p_sb[:, (pidx % 2) * 1024 + u * 512 + tc * 128:
                                 (pidx % 2) * 1024 + u * 512 + (tc + 1) * 128],
                            vaug_view(h2)[:, mc, :],
                            start=(mc == 0), stop=(mc == nmc - 4 + tc))
                sm.tag(last, "pav")

            for pidx, (h, ti, pr, nmc) in enumerate(pair_list):
                h2 = h % 2
                if pr == 0:
                    if ti == 0:
                        sm.wait(t, "akl%d" % (h % 2), 16 * 3 * (h // 2 + 1))
                        if h == 0:
                            sm.wait(t, "vone", 1)
                if pidx >= 2:
                    sm.wait(t, "sexp", pidx - 1)
                last = None
                for u in range(2):
                    mc = 2 * pr + u
                    dd = mc - (nmc - 4)
                    dlt = max(dd, 0) * 128
                    last = t.matmul(
                        spb[pidx % 2][:, u * 512 + dlt:(u + 1) * 512],
                        ksb[:, h2 * S + mc * 128: h2 * S + (mc + 1) * 128],
                        qsb[:, h2 * S + ti * 512 + dlt: h2 * S + (ti + 1) * 512],
                        start=True, stop=True)
                sm.tag(last, "tsc")
                if pr >= 1:
                    av(pidx - 1)
                if pr == nmc // 2 - 1:
                    av(pidx)

        @blk.gpsimd
        def _(g):
            sm.wait(g, "zst0", sm.val("zst0"))
            sm.wait(g, "zst1", sm.val("zst1"))
            cc = g.collective_compute(
                "AllGather", ALU.bypass,
                replica_groups=[[0, 1], [2, 3], [4, 5], [6, 7]],
                ins=[ag_in], outs=[ag_out])
            sm.tag(cc, "ccd")

    if cfg.stop_after == "b2":
        return nc, es
    with (
        nc.sbuf_tensor("w2b", [128, 2 * J], F32) as w2b_g,
        nc.sbuf_tensor("sg2a", [128, J], BF16) as sg2a_g,
        nc.sbuf_tensor("sg2b", [128, J], BF16) as sg2b_g,
        nc.sbuf_tensor("r2tm", [128, 2 * J], BF16) as r2tm_g,
        nc.sbuf_tensor("r2fmb", [128, 2 * NJ * 128], BF16) as r2fmb_g,
        nc.psum_tensor("pq0", [128, 1024], BF16) as pq0_g,
        nc.psum_tensor("pq1", [128, 1024], BF16) as pq1_g,
        nc.named_scope("b4a"), nc.Block("b4a") as blk,
    ):
        w2b, sg2a, sg2b = w2b_g[:], sg2a_g[:], sg2b_g[:]
        r2tm, r2fmb = r2tm_g[:], r2fmb_g[:]
        pqb = [pq0_g[:], pq1_g[:]]
        QBAT = batches(NJ, 8)
        NQB = len(QBAT)

        @blk.sync
        def _(sync):
            for dc in range(ND):
                if dc >= 2:
                    sm.wait(sync, "sq2", dc - 1)
                sm.tag(sync.dma_start(out=w2b[:, ts(dc % 2, J)],
                                      in_=w2[ts(dc, 128), :]), "w2l%d" % (dc % 2), 16)
                if dc >= 1:
                    sm.wait(sync, "evq", dc * NQB)
                    sm.tag(sync.dma_start(out=r2_sc[dc - 1],
                                          in_=r2fmb[:, ts((dc - 1) % 2, NJ * 128)]),
                           "r2st%d" % ((dc - 1) % 2), 16)
            sm.wait(sync, "evq", ND * NQB)
            sm.tag(sync.dma_start(out=r2_sc[ND - 1],
                                  in_=r2fmb[:, ts((ND - 1) % 2, NJ * 128)]),
                   "r2st%d" % ((ND - 1) % 2), 16)
            sync.wait_ge(sm.get("r2st0"), sm.val("r2st0"))
            sync.wait_ge(sm.get("r2st1"), sm.val("r2st1"))

        @blk.scalar
        def _(s):
            for dc in range(ND):
                sm.wait(s, "sqself", dc)
                sm.wait(s, "w2l%d" % (dc % 2), 16 * (dc // 2 + 1))
                sm.wait(s, "vq2", dc)
                s.activation(out=sg2a, in_=w2b[:, ts(dc % 2, J)], func=AF.Sign,
                             bias=h2n)
                sm.tag(s.activation(out=sg2b, in_=w2b[:, ts(dc % 2, J)],
                                    func=AF.Sign, bias=h2c), "sq2")
                sm.mark(s, "sqself")

        @blk.vector
        def _(v):
            v.memset(r2fmb[:], 0.0)

            def evq(dc):
                for bi, bat in enumerate(QBAT):
                    gq = dc * NQB + bi
                    sm.wait(v, "tq2", gq + 1)
                    if dc >= 2:
                        sm.wait(v, "r2st%d" % (dc % 2), 16 * ((dc - 2) // 2 + 1))
                    lo = bat[0] * 128
                    base = (dc % 2) * NJ * 128
                    if bat[-1] == NJ - 1:
                        hi = bat[-1] * 128
                        if hi > lo:
                            v.tensor_copy(out=r2fmb[:, base + lo: base + hi],
                                          in_=pqb[gq % 2][:, 0:hi - lo])
                        sm.tag(v.tensor_copy(
                            out=r2fmb[0:JLAST, base + hi: base + hi + 128],
                            in_=pqb[gq % 2][0:JLAST, hi - lo: hi - lo + 128]),
                            "evq")
                    else:
                        hi = bat[-1] * 128 + 128
                        sm.tag(v.tensor_copy(out=r2fmb[:, base + lo: base + hi],
                                             in_=pqb[gq % 2][:, 0:hi - lo]), "evq")

            for dc in range(ND):
                sm.wait(v, "vqself", dc)
                sm.wait(v, "sq2", dc + 1)
                if dc >= 2:
                    sm.wait(v, "tq2", (dc - 1) * NQB)
                sm.tag(v.tensor_add(out=r2tm[:, ts(dc % 2, J)], in0=sg2a,
                                    in1=sg2b), "vq2")
                if dc >= 1:
                    evq(dc - 1)
                sm.mark(v, "vqself")
            evq(ND - 1)

        @blk.tensor
        def _(t):
            for dc in range(ND):
                sm.wait(t, "vq2", dc + 1)
                for bi, bat in enumerate(QBAT):
                    gidx = dc * NQB + bi
                    if gidx >= 2:
                        sm.wait(t, "evq", gidx - 1)
                    pb = pqb[gidx % 2]
                    last = None
                    for k, jc in enumerate(bat):
                        rows = 128 if jc < NJ - 1 else JLAST
                        last = t.transpose(pb[0:rows, ts(k, 128)],
                                           r2tm[:, (dc % 2) * J + jc * 128:
                                                (dc % 2) * J + jc * 128 + rows],
                                           ident)
                    sm.tag(last, "tq2")

    if cfg.stop_after == "b4a":
        return nc, es
    with (
        nc.sbuf_tensor("xnow", [128, ND * TOWN], BF16) as xnow_g,
        nc.sbuf_tensor("w13b", [128, 4 * D], F32) as w13b_g,
        nc.sbuf_tensor("sg1a", [128, D], BF16) as sg1a_g,
        nc.sbuf_tensor("sg1b", [128, D], BF16) as sg1b_g,
        nc.sbuf_tensor("r13tm", [128, 4 * D], BF16) as r13tm_g,
        nc.sbuf_tensor("r13fm", [128, 2 * 2 * D], BF16) as r13fm_g,
        nc.sbuf_tensor("shb", [128, 2 * 512], F32) as shb_g,
        nc.sbuf_tensor("gb", [128, 2 * TOWN], BF16) as gb_g,
        nc.psum_tensor("pf0", [128, 1024], BF16) as pf0_g,
        nc.psum_tensor("pf1", [128, 1024], BF16) as pf1_g,
        nc.psum_tensor("h1p0", [128, 512], F32) as h1p0_g,
        nc.psum_tensor("h3p0", [128, 512], F32) as h3p0_g,
        nc.psum_tensor("h1p1", [128, 512], F32) as h1p1_g,
        nc.psum_tensor("h3p1", [128, 512], F32) as h3p1_g,
        nc.named_scope("b4b"), nc.Block("b4b") as blk,
    ):
        xnow, w13b = xnow_g[:], w13b_g[:]
        sg1a, sg1b, r13tm, r13fm = sg1a_g[:], sg1b_g[:], r13tm_g[:], r13fm_g[:]
        shb, gb = shb_g[:], gb_g[:]
        pfb = [pf0_g[:], pf1_g[:]]
        h1p = [h1p0_g[:], h1p1_g[:]]
        h3p = [h3p0_g[:], h3p1_g[:]]
        FBAT = batches(2 * ND, 8)
        NFB = len(FBAT)

        @blk.sync
        def _(sync):
            for c in range(ND):
                sm.tag(sync.dma_start(out=xnow[:, ts(c, TOWN)], in_=xn_own_sc[c]),
                       "xol", 16)
            for jc in range(NJ):
                rows = 128 if jc < NJ - 1 else JLAST
                if jc >= 2:
                    sm.wait(sync, "sgn1", 2 * (jc - 1))
                sm.tag(sync.dma_start(
                    out=w13b[0:rows, (jc % 2) * 2 * D:(jc % 2) * 2 * D + D],
                    in_=w1[jc * 128: jc * 128 + rows, :]), "w13l%d" % (jc % 2), 16)
                sm.tag(sync.dma_start(
                    out=w13b[0:rows, (jc % 2) * 2 * D + D:(jc % 2) * 2 * D + 2 * D],
                    in_=w3[jc * 128: jc * 128 + rows, :]), "w13l%d" % (jc % 2), 16)
                if jc >= 1:
                    pj = jc - 1
                    sm.wait(sync, "gev", (pj + 1) * NTT)
                    sm.tag(sync.dma_start(out=g_sc[pj],
                                          in_=gb[:, ts(pj % 2, TOWN)]),
                           "gst%d" % (pj % 2), 16)
            pj = NJ - 1
            sm.wait(sync, "gev", (pj + 1) * NTT)
            sm.tag(sync.dma_start(out=g_sc[pj],
                                  in_=gb[:, ts(pj % 2, TOWN)]),
                   "gst%d" % (pj % 2), 16)
            sync.wait_ge(sm.get("gst0"), sm.val("gst0"))
            sync.wait_ge(sm.get("gst1"), sm.val("gst1"))

        @blk.scalar
        def _(s):
            for jc in range(NJ):
                rows = 128 if jc < NJ - 1 else JLAST
                sm.wait(s, "sfself", jc)
                sm.wait(s, "w13l%d" % (jc % 2), 32 * (jc // 2 + 1))
                sm.wait(s, "vr1", 2 * jc)
                sl = (jc % 2) * 2 * D
                s.activation(out=sg1a[0:rows, :], in_=w13b[0:rows, sl:sl + D],
                             func=AF.Sign, bias=h1n[0:rows, :])
                sm.tag(s.activation(out=sg1b[0:rows, :],
                                    in_=w13b[0:rows, sl:sl + D],
                                    func=AF.Sign, bias=h1c[0:rows, :]), "sgn1")
                sm.wait(s, "vr1", 2 * jc + 1)
                s.activation(out=sg1a[0:rows, :],
                             in_=w13b[0:rows, sl + D:sl + 2 * D],
                             func=AF.Sign, bias=h3n[0:rows, :])
                sm.tag(s.activation(out=sg1b[0:rows, :],
                                    in_=w13b[0:rows, sl + D:sl + 2 * D],
                                    func=AF.Sign, bias=h3c[0:rows, :]), "sgn1")
                for tt in range(NTT):
                    idx = jc * NTT + tt
                    sm.wait(s, "th1", idx + 1)
                    if idx >= 2:
                        sm.wait(s, "gev", idx - 1)
                    sm.tag(s.activation(out=shb[0:rows, ts(tt % 2, 512)],
                                        in_=h1p[tt % 2][0:rows, :], func=AF.Silu,
                                        scale=h1c[0:rows, :]), "ssil")
                sm.mark(s, "sfself")

        @blk.vector
        def _(v):
            v.memset(gb[:], 0.0)

            def evf(jc):
                rows = 128 if jc < NJ - 1 else JLAST
                for bi, bat in enumerate(FBAT):
                    gf = jc * NFB + bi
                    sm.wait(v, "tr13", gf + 1)
                    if jc >= 2:
                        sm.wait(v, "th3", (jc - 1) * NTT)
                    base = (jc % 2) * 2 * D
                    if rows == 128:
                        lo, hi = bat[0] * 128, bat[-1] * 128 + 128
                        sm.tag(v.tensor_copy(out=r13fm[:, base + lo: base + hi],
                                             in_=pfb[gf % 2][:, 0:hi - lo]), "evf")
                    else:
                        last = None
                        for k, c in enumerate(bat):
                            last = v.tensor_copy(
                                out=r13fm[:, base + c * 128: base + c * 128 + rows],
                                in_=pfb[gf % 2][:, k * 128: k * 128 + rows])
                        sm.tag(last, "evf")

            for jc in range(NJ):
                rows = 128 if jc < NJ - 1 else JLAST
                sm.wait(v, "vfself", jc)
                if jc >= 2:
                    sm.wait(v, "gst%d" % (jc % 2), 16 * ((jc - 2) // 2 + 1))
                sl = (jc % 2) * 2 * D
                for half in range(2):
                    sm.wait(v, "sgn1", 2 * jc + half + 1)
                    if jc >= 1:
                        sm.wait(v, "tr13", jc * NFB)
                    sm.tag(v.tensor_add(out=r13tm[0:rows, sl + half * D:
                                                  sl + (half + 1) * D],
                                        in0=sg1a[0:rows, :], in1=sg1b[0:rows, :]),
                           "vr1")
                evf(jc)
                for tt in range(NTT):
                    idx = jc * NTT + tt
                    sm.wait(v, "ssil", idx + 1)
                    sm.wait(v, "th3", idx + 1)
                    sm.tag(v.scalar_tensor_tensor(
                        out=gb[0:rows, (jc % 2) * TOWN + tt * 512:
                               (jc % 2) * TOWN + (tt + 1) * 512],
                        in0=h3p[tt % 2][0:rows, :], scalar=h3c[0:rows, :],
                        in1=shb[0:rows, ts(tt % 2, 512)],
                        op0=ALU.mult, op1=ALU.mult), "gev")
                sm.mark(v, "vfself")

        @blk.tensor
        def _(t):
            sm.wait(t, "xol", 16 * ND)
            for jc in range(NJ):
                rows = 128 if jc < NJ - 1 else JLAST
                sm.wait(t, "vr1", 2 * jc + 2)
                for bi, bat in enumerate(FBAT):
                    gidx = jc * NFB + bi
                    if gidx >= 2:
                        sm.wait(t, "evf", gidx - 1)
                    pb = pfb[gidx % 2]
                    last = None
                    for k, c in enumerate(bat):
                        last = t.transpose(pb[0:128, k * 128: k * 128 + rows],
                                           r13tm[0:rows,
                                                 (jc % 2) * 2 * D + c * 128:
                                                 (jc % 2) * 2 * D + (c + 1) * 128],
                                           ident[0:rows, 0:rows])
                    sm.tag(last, "tr13")
                sm.wait(t, "evf", (jc + 1) * NFB)
                for tt in range(NTT):
                    idx = jc * NTT + tt
                    if idx >= 2:
                        sm.wait(t, "ssil", idx - 1)
                        sm.wait(t, "gev", idx - 1)
                    last = None
                    for c in range(ND):
                        last = t.matmul(h1p[tt % 2][0:rows, :],
                                        r13fm[:, (jc % 2) * 2 * D + c * 128:
                                              (jc % 2) * 2 * D + c * 128 + rows],
                                        xnow[:, c * TOWN + tt * 512:
                                             c * TOWN + tt * 512 + 512],
                                        start=(c == 0), stop=(c == ND - 1))
                    sm.tag(last, "th1")
                    last = None
                    for c in range(ND):
                        last = t.matmul(h3p[tt % 2][0:rows, :],
                                        r13fm[:, (jc % 2) * 2 * D + D + c * 128:
                                              (jc % 2) * 2 * D + D + c * 128 + rows],
                                        xnow[:, c * TOWN + tt * 512:
                                             c * TOWN + tt * 512 + 512],
                                        start=(c == 0), stop=(c == ND - 1))
                    sm.tag(last, "th3")

    if cfg.stop_after == "b4b":
        return nc, es
    with (
        nc.sbuf_tensor("wonat", [128, 2 * ND * 128], BF16) as wonat_g,
        nc.sbuf_tensor("wot", [128, NH * D], BF16) as wot_g,
        nc.sbuf_tensor("z16", [128, 2 * NH * 128], BF16) as z16_g,
        nc.sbuf_tensor("attnb", [128, 4 * 512], F32) as attnb_g,
        nc.psum_tensor("po0", [128, 1024], BF16) as po0_g,
        nc.psum_tensor("po1", [128, 1024], BF16) as po1_g,
        nc.psum_tensor("ap0", [128, 512], F32) as ap0_g,
        nc.psum_tensor("ap1", [128, 512], F32) as ap1_g,
        nc.psum_tensor("ap2", [128, 512], F32) as ap2_g,
        nc.psum_tensor("ap3", [128, 512], F32) as ap3_g,
        nc.named_scope("b3"), nc.Block("b3") as blk,
    ):
        wonat, wot, z16, attnb = wonat_g[:], wot_g[:], z16_g[:], attnb_g[:]
        pob = [po0_g[:], po1_g[:]]
        apb = [ap0_g[:], ap1_g[:], ap2_g[:], ap3_g[:]]
        OBAT = batches(ND, 8)
        NOB = len(OBAT)

        @blk.gpsimd
        def _(g):
            for gh in range(NH):
                if gh >= 2:
                    sm.wait(g, "wotr", (gh - 1) * NOB)
                sm.tag(g.dma_start(
                    out=wonat[:, (gh % 2) * ND * 128:(gh % 2 + 1) * ND * 128]
                        .rearrange("p (c e) -> p c e", c=ND),
                    in_=wo[gh].rearrange("(c p) e -> p c e", p=128)),
                    "wol%d" % (gh % 2), 16)
            sm.wait(g, "ccd", 1)
            ovs = [g.value_load(soff[0:1, tc:tc + 1]) for tc in range(NTC)]
            for tc in range(NTC):
                if tc >= 2:
                    sm.wait(g, "tao", (tc - 1) * NDS)
                src = ag_out[:, :, bass.ds(ovs[tc], 128)].rearrange("g p c -> p g c")
                sm.tag(g.dma_start(
                    out=z16[:, (tc % 2) * NH * 128:(tc % 2 + 1) * NH * 128]
                        .rearrange("p (g c) -> p g c", g=NH),
                    in_=src), "z16l%d" % (tc % 2), 16)

        @blk.tensor
        def _(t):
            for gh in range(NH):
                sm.wait(t, "wol%d" % (gh % 2), 16 * (gh // 2 + 1))
                for bi, bat in enumerate(OBAT):
                    gidx = gh * NOB + bi
                    if gidx >= 2:
                        sm.wait(t, "woev", gidx - 1)
                    pb = pob[gidx % 2]
                    last = None
                    for k, c in enumerate(bat):
                        last = t.transpose(
                            pb[:, ts(k, 128)],
                            wonat[:, (gh % 2) * ND * 128 + c * 128:
                                  (gh % 2) * ND * 128 + (c + 1) * 128],
                            ident)
                    sm.tag(last, "wotr")
            sm.wait(t, "woev", NH * NOB)
            for tc in range(NTC):
                sm.wait(t, "z16l%d" % (tc % 2), 16 * (tc // 2 + 1))
                for ds_ in range(NDS):
                    idx = tc * NDS + ds_
                    if idx >= 4:
                        sm.wait(t, "aoev", idx - 3)
                    pb = apb[idx % 4]
                    last = None
                    for gh in range(NH):
                        last = t.matmul(pb,
                                        z16[:, (tc % 2) * NH * 128 + gh * 128:
                                            (tc % 2) * NH * 128 + (gh + 1) * 128],
                                        wot[:, gh * D + ds_ * 512:
                                            gh * D + ds_ * 512 + 512],
                                        start=(gh == 0), stop=(gh == NH - 1))
                    sm.tag(last, "tao")

        @blk.scalar
        def _(s):
            for gh in range(NH):
                for bi, bat in enumerate(OBAT):
                    gidx = gh * NOB + bi
                    sm.wait(s, "wotr", gidx + 1)
                    ncols = len(bat) * 128
                    sm.tag(s.copy(out=wot[:, gh * D + bat[0] * 128:
                                          gh * D + bat[0] * 128 + ncols],
                                  in_=pob[gidx % 2][:, 0:ncols]), "woev")

        @blk.vector
        def _(v):
            for tc in range(NTC):
                for ds_ in range(NDS):
                    idx = tc * NDS + ds_
                    sm.wait(v, "tao", idx + 1)
                    if idx >= 4:
                        sm.wait(v, "aost%d" % (idx % 4), 16 * ((idx - 4) // 4 + 1))
                    sm.tag(v.tensor_copy(out=attnb[:, ts(idx % 4, 512)],
                                         in_=apb[idx % 4]), "aoev")

        @blk.sync
        def _(sync):
            for tc in range(NTC):
                for ds_ in range(NDS):
                    idx = tc * NDS + ds_
                    sm.wait(sync, "aoev", idx + 1)
                    sm.tag(sync.dma_start(out=attn_sc[ts(tc, 128), ts(ds_, 512)],
                                          in_=attnb[:, ts(idx % 4, 512)]),
                           "aost%d" % (idx % 4), 16)
            for q in range(4):
                sync.wait_ge(sm.get("aost%d" % q), sm.val("aost%d" % q))

    if cfg.stop_after == "b3":
        return nc, es
    NTG = TOWN // 256
    with (
        nc.sbuf_tensor("r2sb", [128, NDH * NJ * 128], BF16) as r2sb_g,
        nc.sbuf_tensor("g2sb", [128, 2 * NJ * 256], BF16) as g2sb_g,
        nc.sbuf_tensor("axb", [128, 4 * NDH * 128], F32) as axb_g,
        nc.sbuf_tensor("outb", [128, 2 * NDH * 128], F32) as outb_g,
        nc.psum_tensor("fo0", [128, 512], F32) as fo0_g,
        nc.psum_tensor("fo1", [128, 512], F32) as fo1_g,
        nc.psum_tensor("fo2", [128, 512], F32) as fo2_g,
        nc.psum_tensor("fo3", [128, 512], F32) as fo3_g,
        nc.psum_tensor("fo4", [128, 512], F32) as fo4_g,
        nc.psum_tensor("fo5", [128, 512], F32) as fo5_g,
        nc.psum_tensor("fo6", [128, 512], F32) as fo6_g,
        nc.psum_tensor("fo7", [128, 512], F32) as fo7_g,
        nc.named_scope("b5"), nc.Block("b5") as blk,
    ):
        r2sb, g2sb, axb, outb = r2sb_g[:], g2sb_g[:], axb_g[:], outb_g[:]
        fob = [fo0_g[:], fo1_g[:], fo2_g[:], fo3_g[:],
               fo4_g[:], fo5_g[:], fo6_g[:], fo7_g[:]]
        HW = NDH * 128

        @blk.sync
        def _(sync):
            for dh in range(2):
                if dh == 1:
                    sm.wait(sync, "tfo", NTG)
                for k in range(NDH):
                    if "r2" in cfg.b5_skip:
                        break
                    sm.tag(sync.dma_start(out=r2sb[:, ts(k, NJ * 128)],
                                          in_=r2_sc[dh * NDH + k]), "r2l", 16)
                for tg in range(NTG):
                    gidx = dh * NTG + tg
                    if gidx >= 2:
                        sm.wait(sync, "tfo", gidx - 1)
                    for jb in batches(NJ, 6):
                        if "g2" in cfg.b5_skip:
                            break
                        j0, jn = jb[0], len(jb)
                        src = g_sc[j0:j0 + jn, :, ts(tg, 256)].rearrange(
                            "j p c -> p j c")
                        sm.tag(sync.dma_start(
                            out=g2sb[:, (gidx % 2) * NJ * 256 + j0 * 256:
                                     (gidx % 2) * NJ * 256 + (j0 + jn) * 256]
                                .rearrange("p (j c) -> p j c", j=jn),
                            in_=src), "g2l%d" % (gidx % 2), 16)
                    for u in range(2):
                        tc = 2 * tg + u
                        gi = dh * NTC + tc
                        if gi >= 2:
                            sm.wait(sync, "foev", (gi - 1) * NDSS)
                        if "ax" not in cfg.b5_skip:
                            sm.tag(sync.dma_start(
                                out=axb[:, (gi % 2) * 2 * HW:
                                        (gi % 2) * 2 * HW + HW],
                                in_=attn_sc[ts(tc, 128), dh * HW:(dh + 1) * HW]),
                                "axl%d" % (gi % 2), 16)
                            sm.tag(sync.dma_start(
                                out=axb[:, (gi % 2) * 2 * HW + HW:
                                        (gi % 2 + 1) * 2 * HW],
                                in_=xo[ts(tc, 128), dh * HW:(dh + 1) * HW]),
                                "axl%d" % (gi % 2), 16)
                        if gi >= 1:
                            pgi = gi - 1
                            sm.wait(sync, "foev", (pgi + 1) * NDSS)
                            sm.tag(sync.dma_start(
                                out=out[ts(pgi % NTC, 128),
                                        (pgi // NTC) * HW:(pgi // NTC + 1) * HW],
                                in_=outb[:, ts(pgi % 2, HW)]),
                                "fst%d" % (pgi % 2), 16)
            pgi = 2 * NTC - 1
            sm.wait(sync, "foev", (pgi + 1) * NDSS)
            sm.tag(sync.dma_start(
                out=out[ts(pgi % NTC, 128),
                        (pgi // NTC) * HW:(pgi // NTC + 1) * HW],
                in_=outb[:, ts(pgi % 2, HW)]), "fst%d" % (pgi % 2), 16)
            sync.wait_ge(sm.get("fst0"), sm.val("fst0"))
            sync.wait_ge(sm.get("fst1"), sm.val("fst1"))

        @blk.tensor
        def _(t):
            r2v = r2sb.rearrange("p (k x) -> p k x", k=NDH)
            for dh in range(2):
                if "r2" not in cfg.b5_skip:
                    sm.wait(t, "r2l", 16 * NDH * (dh + 1))
                for tg in range(NTG):
                    gidx = dh * NTG + tg
                    if "g2" not in cfg.b5_skip:
                        sm.wait(t, "g2l%d" % (gidx % 2),
                                16 * len(batches(NJ, 6)) * (gidx // 2 + 1))
                    grp = gidx % 2
                    bank0 = (tg % 2) * 2 * NDSS
                    if gidx >= 2:
                        sm.wait(t, "foev", (gidx - 1) * 2 * NDSS)
                    if "mm" in cfg.b5_skip:
                        sm.mark(t, "tfo")
                    else:
                        last = None
                        for jc in range(NJ):
                            rows = 128 if jc < NJ - 1 else JLAST
                            for u in range(2):
                                for dss in range(NDSS):
                                    rhs = r2v[:, dss * 4:(dss + 1) * 4,
                                              jc * 128: jc * 128 + 128]
                                    last = t.matmul(
                                        fob[bank0 + u * NDSS + dss],
                                        g2sb[0:rows,
                                             grp * NJ * 256 + jc * 256 + u * 128:
                                             grp * NJ * 256 + jc * 256 + (u + 1) * 128],
                                        rhs[0:rows, :, :],
                                        start=(jc == 0), stop=(jc == NJ - 1))
                        sm.tag(last, "tfo")

        @blk.vector
        def _(v):
            for dh in range(2):
                for tg in range(NTG):
                    gidx = dh * NTG + tg
                    sm.wait(v, "tfo", gidx + 1)
                    bank0 = (tg % 2) * 2 * NDSS
                    for u in range(2):
                        gi = dh * NTC + tg * 2 + u
                        if "ax" not in cfg.b5_skip:
                            sm.wait(v, "axl%d" % (gi % 2), 32 * (gi // 2 + 1))
                        if gi >= 2:
                            sm.wait(v, "fst%d" % (gi % 2), 16 * ((gi - 2) // 2 + 1))
                        for dss in range(NDSS):
                            o = outb[:, (gi % 2) * HW + dss * 512:
                                     (gi % 2) * HW + (dss + 1) * 512]
                            if "vec" in cfg.b5_skip:
                                v.memset(o, 0.0)
                                sm.mark(v, "foev")
                                continue
                            v.scalar_tensor_tensor(
                                out=o, in0=fob[bank0 + u * NDSS + dss], scalar=h2c,
                                in1=axb[:, (gi % 2) * 2 * HW + dss * 512:
                                        (gi % 2) * 2 * HW + (dss + 1) * 512],
                                op0=ALU.mult, op1=ALU.add)
                            sm.tag(v.tensor_add(
                                out=o, in0=o,
                                in1=axb[:, (gi % 2) * 2 * HW + HW + dss * 512:
                                        (gi % 2) * 2 * HW + HW + (dss + 1) * 512]),
                                "foev")

    return nc, es


def kernel(L0_attn_x, L0_attn_W_Q, L0_attn_W_K, L0_attn_W_V, L0_attn_W_O,
           L0_ffn_w1, L0_ffn_w3, L0_ffn_w2):
    cfg = Cfg()
    x = np.ascontiguousarray(np.asarray(L0_attn_x, dtype=np.float32))
    WQ = np.ascontiguousarray(np.asarray(L0_attn_W_Q, dtype=np.float32))
    WK = np.ascontiguousarray(np.asarray(L0_attn_W_K, dtype=np.float32))
    WV = np.ascontiguousarray(np.asarray(L0_attn_W_V, dtype=np.float32))
    WO = np.ascontiguousarray(np.asarray(L0_attn_W_O, dtype=np.float32))
    w1 = np.ascontiguousarray(np.asarray(L0_ffn_w1, dtype=np.float32))
    w3 = np.ascontiguousarray(np.asarray(L0_ffn_w3, dtype=np.float32))
    w2 = np.ascontiguousarray(np.asarray(L0_ffn_w2, dtype=np.float32))

    nc, es = build(cfg)
    in_maps = []
    for c in range(8):
        b, hh = c // 2, c % 2
        offs = np.arange(cfg.NTC, dtype=np.int32) * 128 + hh * cfg.TOWN
        in_maps.append({
            "xk": x[b],
            "xo": np.ascontiguousarray(x[b, hh * cfg.TOWN:(hh + 1) * cfg.TOWN]),
            "wq": np.ascontiguousarray(WQ[hh * cfg.HC:(hh + 1) * cfg.HC]),
            "wk": np.ascontiguousarray(WK[hh * cfg.HC:(hh + 1) * cfg.HC]),
            "wv": np.ascontiguousarray(WV[hh * cfg.HC:(hh + 1) * cfg.HC]),
            "wo": WO,
            "w1": w1, "w3": w3, "w2": w2,
            "roff": offs.reshape(1, cfg.NTC),
        })
    res = run_bass_kernel_spmd(nc, in_maps, core_ids=list(range(8)))
    es.close()
    outp = np.empty((4, cfg.S, cfg.D), dtype=np.float32)
    for c in range(8):
        b, hh = c // 2, c % 2
        outp[b, hh * cfg.TOWN:(hh + 1) * cfg.TOWN] = res.results[c]["out"]
    return outp


# version 07
# speedup vs baseline: 1.0076x; 1.0076x over previous
import math
import numpy as np
import ml_dtypes
from contextlib import ExitStack

import concourse.bass as bass
import concourse.mybir as mybir
from concourse.bass_utils import run_bass_kernel_spmd

F32 = mybir.dt.float32
BF16 = mybir.dt.bfloat16
I32 = mybir.dt.int32
AF = mybir.ActivationFunctionType
ALU = mybir.AluOpType
AX = mybir.AxisListType

RSCALE = 1.0 / math.sqrt(128.0)


def ts(i, n):
    return bass.ts(i, n)


class Cfg:
    def __init__(self, D=2048, S=4096, HC=8, J=5461, NH=16, stop_after=None,
                 b5_skip=(), debug_out=False):
        self.stop_after = stop_after
        self.b5_skip = set(b5_skip)
        self.debug_out = debug_out
        self.D, self.S, self.HC, self.J, self.NH = D, S, HC, J, NH
        self.E = 128
        self.ND = D // 128
        self.NMT = S // 512
        self.NCK = S // 128
        self.TOWN = S // 2
        self.NTC = self.TOWN // 128
        self.NTT = self.TOWN // 512
        self.NJ = (J + 127) // 128
        self.JLAST = J - (self.NJ - 1) * 128
        self.NDH = self.ND // 2
        self.NDSS = (self.NDH * 128) // 512
        assert D % 1024 == 0 and S % 512 == 0 and NH == 2 * HC


class Sems:
    def __init__(self, nc, es):
        self.nc, self.es, self.h, self.n = nc, es, {}, {}

    def get(self, name):
        if name not in self.h:
            self.h[name] = self.es.enter_context(self.nc.semaphore(name))
            self.n[name] = 0
        return self.h[name]

    def tag(self, instr, name, k=1):
        instr.then_inc(self.get(name), k)
        self.n[name] += k
        return self.n[name]

    def mark(self, eng, name, k=1):
        return self.tag(eng.drain(), name, k)

    def val(self, name):
        self.get(name)
        return self.n[name]

    def wait(self, eng, name, v):
        if v > 0:
            eng.wait_ge(self.get(name), v)


def batches(n, size=8):
    out, i = [], 0
    while i < n:
        out.append(list(range(i, min(i + size, n))))
        i += size
    return out


def build(cfg: Cfg):
    nc = bass.Bass("TRN2", target_bir_lowering=False, detect_race_conditions=False)
    D, S, HC, E, ND = cfg.D, cfg.S, cfg.HC, cfg.E, cfg.ND
    NMT, NCK, TOWN, NTC, NTT = cfg.NMT, cfg.NCK, cfg.TOWN, cfg.NTC, cfg.NTT
    J, NJ, JLAST, NH = cfg.J, cfg.NJ, cfg.JLAST, cfg.NH
    NDH, NDSS = cfg.NDH, cfg.NDSS
    NW = float(J * D)
    NDS = D // 512

    xk = nc.dram_tensor("xk", [S, D], F32, kind="ExternalInput").ap()
    xo = nc.dram_tensor("xo", [TOWN, D], F32, kind="ExternalInput").ap()
    wq = nc.dram_tensor("wq", [HC, E, D], F32, kind="ExternalInput").ap()
    wk = nc.dram_tensor("wk", [HC, E, D], F32, kind="ExternalInput").ap()
    wv = nc.dram_tensor("wv", [HC, E, D], F32, kind="ExternalInput").ap()
    wo = nc.dram_tensor("wo", [NH, D, E], F32, kind="ExternalInput").ap()
    w1 = nc.dram_tensor("w1", [J, D], F32, kind="ExternalInput").ap()
    w3 = nc.dram_tensor("w3", [J, D], F32, kind="ExternalInput").ap()
    w2 = nc.dram_tensor("w2", [D, J], F32, kind="ExternalInput").ap()
    roff = nc.dram_tensor("roff", [1, NTC], I32, kind="ExternalInput").ap()
    out = nc.dram_tensor("out", [TOWN, D], F32, kind="ExternalOutput").ap()

    dk = "ExternalOutput" if cfg.debug_out else "Internal"
    xn_dbg = (nc.dram_tensor("xn_dbg", [NCK + TOWN // 128, 128, D], BF16,
                             kind="ExternalOutput").ap() if cfg.debug_out else None)
    ln_dbg = (nc.dram_tensor("ln_dbg", [NCK + TOWN // 128, 128, 18], F32,
                             kind="ExternalOutput").ap() if cfg.debug_out else None)
    xn_own_sc = nc.dram_tensor("xn_own_sc", [ND, 128, TOWN], BF16, kind=dk).ap()
    k_sc = nc.dram_tensor("k_sc", [HC, E, S], BF16, kind=dk).ap()
    q_sc = nc.dram_tensor("q_sc", [HC, E, S], BF16, kind=dk).ap()
    v_sc = nc.dram_tensor("v_sc", [HC, 128, S], BF16, kind=dk).ap()
    ag_in = nc.dram_tensor("ag_in", [HC, E, S], BF16, kind="Internal").ap()
    ag_out = nc.dram_tensor("ag_out", [NH, E, S], BF16, kind="Internal").ap()
    g_sc = nc.dram_tensor("g_sc", [NJ, 128, TOWN], BF16, kind=dk).ap()
    r2_sc = nc.dram_tensor("r2_sc", [ND, 128, NJ * 128], BF16, kind=dk).ap()
    attn_sc = nc.dram_tensor("attn_sc", [TOWN, D], F32, kind=dk).ap()

    es = ExitStack()
    sm = Sems(nc, es)

    ident = nc.alloc_sbuf_tensor("ident", [128, 128], BF16).ap()
    maskst = nc.alloc_sbuf_tensor("maskst", [128, 128], F32).ap()
    ones_f32 = nc.alloc_sbuf_tensor("ones_f32", [128, 128], F32).ap()
    soff = nc.alloc_sbuf_tensor("soff", [1, NTC], I32).ap()

    SCH = 128 * 1024
    NWI = J * D
    NST1 = (NWI + SCH - 1) // SCH
    stats_plan = ([("w1", j) for j in range(NST1)] + [("w3", j) for j in range(NST1)]
                  + [("w2", j) for j in range(NST1)])
    NSTAT = len(stats_plan)
    s1cols = nc.alloc_sbuf_tensor("s1cols", [128, NST1], F32).ap()
    s3cols = nc.alloc_sbuf_tensor("s3cols", [128, NST1], F32).ap()
    s2cols = nc.alloc_sbuf_tensor("s2cols", [128, NST1], F32).ap()
    h1c = nc.alloc_sbuf_tensor("h1c", [128, 1], F32).ap()
    h3c = nc.alloc_sbuf_tensor("h3c", [128, 1], F32).ap()
    h2c = nc.alloc_sbuf_tensor("h2c", [128, 1], F32).ap()
    h1n = nc.alloc_sbuf_tensor("h1n", [128, 1], F32).ap()
    h3n = nc.alloc_sbuf_tensor("h3n", [128, 1], F32).ap()
    h2n = nc.alloc_sbuf_tensor("h2n", [128, 1], F32).ap()
    redc = nc.alloc_sbuf_tensor("redc", [128, 3], F32).ap()
    epsc = nc.alloc_sbuf_tensor("epsc", [128, 1], F32).ap()

    def stats_src(idx):
        kind, j = stats_plan[idx]
        t = {"w1": w1, "w3": w3, "w2": w2}[kind]
        flat = t.rearrange("a b -> (a b)")
        lo = j * SCH
        n = min(SCH, NWI - lo)
        p = n // 1024
        rem = n - p * 1024
        main = flat[lo: lo + p * 1024].rearrange("(p c) -> p c", p=p) if p else None
        tail = (flat[lo + p * 1024: lo + n].rearrange("(p c) -> p c", p=1)
                if rem else None)
        return main, tail, p

    def stats_col(idx):
        kind, j = stats_plan[idx]
        return {"w1": s1cols, "w3": s3cols, "w2": s2cols}[kind][:, j:j + 1]

    with nc.named_scope("b0"), nc.Block("b0") as blk:
        @blk.gpsimd
        def _(g):
            g.dma_start(out=soff[:], in_=roff).then_inc(sm.get("gld0"), 16)
            sm.tag(g.memset(ident[:], 1.0), "g0")
            g.wait_ge(sm.get("g0"), 1)
            sm.tag(g.affine_select(out=ident[:], in_=ident[:], compare_op=ALU.is_ge,
                                   fill=0.0, base=0, pattern=[[1, 128]],
                                   channel_multiplier=-1), "g0")
            g.wait_ge(sm.get("g0"), 2)
            sm.tag(g.affine_select(out=ident[:], in_=ident[:], compare_op=ALU.is_ge,
                                   fill=0.0, base=0, pattern=[[-1, 128]],
                                   channel_multiplier=1), "g0")
            sm.tag(g.memset(maskst[:], 0.0), "g0")
            g.wait_ge(sm.get("g0"), 4)
            g.affine_select(out=maskst[:], in_=maskst[:], compare_op=ALU.is_ge,
                            fill=-1e9, base=0, pattern=[[1, 128]],
                            channel_multiplier=-1)
            g.wait_ge(sm.get("gld0"), 16)

        @blk.vector
        def _(v):
            v.memset(ones_f32[:], 1.0)
            v.memset(epsc[:], 1e-5)
            v.memset(s1cols[:], 0.0)
            v.memset(s3cols[:], 0.0)
            v.memset(s2cols[:], 0.0)

    with (
        nc.sbuf_tensor("xn_fm", [128, ND * S], BF16) as xn_fm_g,
        nc.sbuf_tensor("statw", [128, 2 * 1024], F32) as statw_g,
    ):
        xn_fm = xn_fm_g[:]
        statw = statw_g[:]
        stld_at = []
        _cnt = {0: 0, 1: 0}
        for _sdi in range(NSTAT):
            _m, _t, _p = stats_src(_sdi)
            _cnt[_sdi % 2] += 16 * ((1 if _m is not None else 0)
                                    + (1 if _t is not None else 0))
            stld_at.append(("stld%d" % (_sdi % 2), _cnt[_sdi % 2]))

        def stats_dma(g, sdi):
            if sdi >= 2:
                sm.wait(g, "stdone", sdi - 1)
            main, tail, p = stats_src(sdi)
            sem = "stld%d" % (sdi % 2)
            if main is not None:
                sm.tag(g.dma_start(out=statw[0:p, ts(sdi % 2, 1024)], in_=main),
                       sem, 16)
            if tail is not None:
                sm.tag(g.dma_start(out=statw[p:p + 1, (sdi % 2) * 1024:
                                             (sdi % 2) * 1024 + tail.shape[1]],
                                   in_=tail), sem, 16)

        def stats_unit(v, sui):
            sm.wait(v, stld_at[sui][0], stld_at[sui][1])
            main, tail, p = stats_src(sui)
            rows = p + (1 if tail is not None else 0)
            sm.tag(v.tensor_reduce(out=stats_col(sui)[0:rows, :],
                                   in_=statw[0:rows, ts(sui % 2, 1024)],
                                   axis=AX.X, op=ALU.add,
                                   apply_absolute_value=True), "stdone")

        nchunks = NCK + NTC
        with (
            nc.sbuf_tensor("x_sb", [128, 3 * D], F32) as x_sb_g,
            nc.sbuf_tensor("trash", [128, D], BF16) as trash_g,
            nc.sbuf_tensor("xn_tm", [128, 2 * D], BF16) as xn_tm_g,
            nc.sbuf_tensor("obuf", [128, 2 * D], BF16) as obuf_g,
            nc.sbuf_tensor("lncol", [128, 18], F32) as lncol_g,
            nc.psum_tensor("pt0", [128, 1024], BF16) as pt0_g,
            nc.psum_tensor("pt1", [128, 1024], BF16) as pt1_g,
            nc.named_scope("b1a"), nc.Block("b1a") as blk,
        ):
            x_sb, trash = x_sb_g[:], trash_g[:]
            xn_tm, obuf, lncol = xn_tm_g[:], obuf_g[:], lncol_g[:]
            ptb = [pt0_g[:], pt1_g[:]]

            @blk.sync
            def _(sync):
                for i in range(nchunks):
                    if cfg.debug_out and i >= 1:
                        sm.wait(sync, "snm", i)
                        sm.tag(sync.dma_start(out=xn_dbg[i - 1],
                                              in_=xn_tm[:, ts((i - 1) % 2, D)]),
                               "xdbg", 16)
                        sm.tag(sync.dma_start(out=ln_dbg[i - 1], in_=lncol[:]),
                               "xdbg", 16)
                        sm.wait(sync, "xdbg", sm.val("xdbg"))
                    if i >= 3:
                        sm.wait(sync, "vself", i - 2)
                        sm.wait(sync, "sself", i - 2)
                    src = (xk[ts(i, 128), :] if i < NCK else xo[ts(i - NCK, 128), :])
                    sm.tag(sync.dma_start(out=x_sb[:, ts(i % 3, D)], in_=src),
                           "xld%d" % (i % 3), 16)
                    if i >= NCK + 1:
                        oj = i - NCK - 1
                        sm.wait(sync, "tev", (NCK + oj + 1) * ((ND + 7) // 8))
                        sm.tag(sync.dma_start(
                            out=xn_own_sc[:, :, ts(oj, 128)].rearrange(
                                "c p t -> p c t"),
                            in_=obuf[:, ts(oj % 2, D)].rearrange(
                                "p (c e) -> p c e", c=ND)), "ost%d" % (oj % 2), 16)
                if cfg.debug_out:
                    sm.wait(sync, "snm", nchunks)
                    sm.tag(sync.dma_start(out=xn_dbg[nchunks - 1],
                                          in_=xn_tm[:, ts((nchunks - 1) % 2, D)]),
                           "xdbg", 16)
                    sm.tag(sync.dma_start(out=ln_dbg[nchunks - 1], in_=lncol[:]),
                           "xdbg", 16)
                    sm.wait(sync, "xdbg", sm.val("xdbg"))
                oj = NTC - 1
                sm.wait(sync, "tev", (NCK + oj + 1) * ((ND + 7) // 8))
                sm.tag(sync.dma_start(
                    out=xn_own_sc[:, :, ts(oj, 128)].rearrange("c p t -> p c t"),
                    in_=obuf[:, ts(oj % 2, D)].rearrange("p (c e) -> p c e", c=ND)),
                    "ost%d" % (oj % 2), 16)
                sync.wait_ge(sm.get("ost0"), sm.val("ost0"))
                sync.wait_ge(sm.get("ost1"), sm.val("ost1"))

            TBAT = batches(ND, 8)
            NTB = len(TBAT)

            def evac(v, ci):
                if ci >= NCK:
                    oi = ci - NCK
                    if oi >= 2:
                        sm.wait(v, "ost%d" % (oi % 2), 16 * ((oi - 2) // 2 + 1))
                for bi, bat in enumerate(TBAT):
                    gidx = ci * NTB + bi
                    sm.wait(v, "ttr", gidx + 1)
                    pb = ptb[gidx % 2]
                    nb = len(bat)
                    if ci < NCK:
                        dst = xn_fm.rearrange("p (c s) -> p c s", c=ND)[
                            :, bat[0]:bat[0] + nb, ts(ci, 128)]
                    else:
                        oi = ci - NCK
                        dst = obuf[:, ts(oi % 2, D)].rearrange(
                            "p (c e) -> p c e", c=ND)[:, bat[0]:bat[0] + nb, :]
                    sm.tag(v.tensor_copy(out=dst,
                                         in_=pb[:, 0:nb * 128].rearrange(
                                             "p (c e) -> p c e", c=nb)), "tev")

            @blk.vector
            def _(v):
                sui = 0
                for i in range(nchunks):
                    sm.wait(v, "vself", i)
                    sl = i % 3
                    sm.wait(v, "xld%d" % sl, 16 * (i // 3 + 1))
                    c_s = lncol[:, sl:sl + 1]
                    c_m = lncol[:, 3 + sl:4 + sl]
                    c_v = lncol[:, 6 + sl:7 + sl]
                    c_r = lncol[:, 9 + sl:10 + sl]
                    c_b = lncol[:, 12 + sl:13 + sl]
                    c_q = lncol[:, 15 + sl:16 + sl]
                    c1 = sm.tag(v.reduce_sum(out=c_s, in_=x_sb[:, ts(sl, D)],
                                             axis=AX.X), "vrd")
                    sm.wait(v, "vrd", c1)
                    c2 = sm.tag(v.tensor_scalar(out=c_m, in0=c_s, scalar1=1.0 / D,
                                                scalar2=None, op0=ALU.mult), "vrd")
                    sm.wait(v, "ssq", i + 1)
                    sm.wait(v, "vrd", c2)
                    c3 = sm.tag(v.tensor_mul(out=c_v, in0=c_m, in1=c_m), "vrd")
                    sm.wait(v, "vrd", c3)
                    sm.tag(v.scalar_tensor_tensor(out=c_v, in0=c_q, scalar=1.0 / D,
                                                  in1=c_v, op0=ALU.mult,
                                                  op1=ALU.subtract), "vvar")
                    sm.wait(v, "srs", i + 1)
                    sm.tag(v.scalar_tensor_tensor(out=c_b, in0=c_m, scalar=-1.0,
                                                  in1=c_r, op0=ALU.mult,
                                                  op1=ALU.mult), "vnb")
                    if i >= 1:
                        evac(v, i - 1)
                    if sui < NSTAT and i % 2 == 0:
                        stats_unit(v, sui)
                        sui += 1
                    sm.mark(v, "vself")
                evac(v, nchunks - 1)
                sm.n["sui_done"] = sui

            @blk.scalar
            def _(s):
                for i in range(nchunks):
                    sm.wait(s, "sself", i)
                    sl = i % 3
                    sm.wait(s, "xld%d" % sl, 16 * (i // 3 + 1))
                    c_v = lncol[:, 6 + sl:7 + sl]
                    c_r = lncol[:, 9 + sl:10 + sl]
                    c_q = lncol[:, 15 + sl:16 + sl]
                    sm.tag(s.activation(out=trash[:], in_=x_sb[:, ts(sl, D)],
                                        func=AF.Square, accum_out=c_q), "ssq")
                    sm.wait(s, "vvar", i + 1)
                    s.activation(out=c_r, in_=c_v, func=AF.Ln, bias=epsc)
                    sm.tag(s.activation(out=c_r, in_=c_r, func=AF.Exp, scale=-0.5),
                           "srs")
                    sm.wait(s, "vnb", i + 1)
                    if i >= 2:
                        sm.wait(s, "ttr", (i - 1) * ((ND + 7) // 8))
                    sm.tag(s.activation(out=xn_tm[:, ts(i % 2, D)],
                                        in_=x_sb[:, ts(sl, D)], func=AF.Identity,
                                        bias=lncol[:, 12 + sl:13 + sl], scale=c_r),
                           "snm")
                    sm.mark(s, "sself")

            @blk.tensor
            def _(t):
                for i in range(nchunks):
                    sm.wait(t, "snm", i + 1)
                    for bi, bat in enumerate(TBAT):
                        gidx = i * NTB + bi
                        if gidx >= 2:
                            sm.wait(t, "tev", gidx - 1)
                        pb = ptb[gidx % 2]
                        last = None
                        for k, c in enumerate(bat):
                            last = t.transpose(pb[:, ts(k, 128)],
                                               xn_tm[:, (i % 2) * D + c * 128:
                                                     (i % 2) * D + (c + 1) * 128],
                                               ident)
                        sm.tag(last, "ttr")

            @blk.gpsimd
            def _(g):
                nb1a = min((nchunks + 1) // 2, NSTAT)
                for sdi in range(min(nb1a + 2, NSTAT)):
                    stats_dma(g, sdi)
                sm.n["sdi_done"] = min(nb1a + 2, NSTAT)

        if cfg.stop_after == "b1a":
            return nc, es
        with (
            nc.sbuf_tensor("wnat", [128, 2 * 3 * D], BF16) as wnat_g,
            nc.sbuf_tensor("wT", [128, 3 * D], BF16) as wT_g,
            nc.sbuf_tensor("kqb", [128, 4 * 512], BF16) as kqb_g,
            nc.sbuf_tensor("vb", [128, 4 * 512], BF16) as vb_g,
            nc.psum_tensor("pw0", [128, 1024], BF16) as pw0_g,
            nc.psum_tensor("pw1", [128, 1024], BF16) as pw1_g,
            nc.psum_tensor("kp0", [128, 512], F32) as kp0_g,
            nc.psum_tensor("kp1", [128, 512], F32) as kp1_g,
            nc.psum_tensor("vp0", [128, 512], F32) as vp0_g,
            nc.psum_tensor("vp1", [128, 512], F32) as vp1_g,
            nc.named_scope("b1b"), nc.Block("b1b") as blk,
        ):
            wnat, wT, kqb, vb = wnat_g[:], wT_g[:], kqb_g[:], vb_g[:]
            pwb = [pw0_g[:], pw1_g[:]]
            kpb = [kp0_g[:], kp1_g[:]]
            vpb = [vp0_g[:], vp1_g[:]]
            WBAT = batches(3 * ND, 8)
            NWB = len(WBAT)
            NVG = NCK // 4

            @blk.gpsimd
            def _(g):
                sdi = sm.n.get("sdi_done", NSTAT)
                for h in range(HC):
                    if h >= 2:
                        sm.wait(g, "twtr", (h - 1) * NWB)
                    for mi, wsrc in enumerate((wq, wk, wv)):
                        sm.tag(g.dma_start(
                            out=wnat[:, (h % 2) * 3 * D + mi * D:
                                     (h % 2) * 3 * D + (mi + 1) * D],
                            in_=wsrc[h]), "gwl%d" % (h % 2), 16)
                    quota = (NSTAT - sm.n.get("sdi_done", NSTAT) + HC - 1) // HC
                    for _ in range(quota):
                        if sdi < NSTAT:
                            stats_dma(g, sdi)
                            sdi += 1
                while sdi < NSTAT:
                    stats_dma(g, sdi)
                    sdi += 1

            @blk.tensor
            def _(t):
                for h in range(HC):
                    sm.wait(t, "gwl%d" % (h % 2), 16 * 3 * (h // 2 + 1))
                    base = (h % 2) * 3 * D
                    for bi, bat in enumerate(WBAT):
                        gidx = h * NWB + bi
                        if gidx >= 2:
                            sm.wait(t, "twev", gidx - 1)
                        pb = pwb[gidx % 2]
                        last = None
                        for k, c in enumerate(bat):
                            last = t.transpose(pb[:, ts(k, 128)],
                                               wnat[:, base + c * 128:
                                                    base + (c + 1) * 128], ident)
                        sm.tag(last, "twtr")
                    sm.wait(t, "twev", (h + 1) * NWB)
                    for mi in range(2):
                        woff = (1 - mi) * D
                        for mt in range(NMT):
                            idx = h * 2 * NMT + mi * NMT + mt
                            if idx >= 2:
                                sm.wait(t, "skv", idx - 1)
                            pb = kpb[idx % 2]
                            last = None
                            for c in range(ND):
                                last = t.matmul(pb,
                                                wT[:, woff + c * 128:
                                                   woff + (c + 1) * 128],
                                                xn_fm[:, c * S + mt * 512:
                                                      c * S + mt * 512 + 512],
                                                start=(c == 0), stop=(c == ND - 1))
                            sm.tag(last, "tkq")
                    for vg in range(NVG):
                        idx = h * NVG + vg
                        if idx >= 2:
                            sm.wait(t, "svv", idx - 1)
                        pb = vpb[idx % 2]
                        last = None
                        for u in range(4):
                            msub = vg * 4 + u
                            for c in range(ND):
                                last = t.matmul(pb[:, ts(u, 128)],
                                                xn_fm[:, c * S + msub * 128:
                                                      c * S + (msub + 1) * 128],
                                                wT[:, 2 * D + c * 128:
                                                   2 * D + (c + 1) * 128],
                                                start=(c == 0), stop=(c == ND - 1))
                        sm.tag(last, "tvv")

            @blk.scalar
            def _(s):
                for h in range(HC):
                    sm.wait(s, "sbself", h)
                    if h >= 1:
                        sm.wait(s, "tvv", h * NVG)
                    for bi, bat in enumerate(WBAT):
                        gidx = h * NWB + bi
                        sm.wait(s, "twtr", gidx + 1)
                        ncols = len(bat) * 128
                        sm.tag(s.copy(out=wT[:, bat[0] * 128: bat[0] * 128 + ncols],
                                      in_=pwb[gidx % 2][:, 0:ncols]), "twev")
                    for mi in range(2):
                        for mt in range(NMT):
                            idx = h * 2 * NMT + mi * NMT + mt
                            sm.wait(s, "tkq", idx + 1)
                            if idx >= 4:
                                sm.wait(s, "kqst%d" % (idx % 4),
                                        16 * ((idx - 4) // 4 + 1))
                            slot = kqb[:, ts(idx % 4, 512)]
                            if mi == 1:
                                ins = s.activation(out=slot, in_=kpb[idx % 2],
                                                   func=AF.Copy, scale=RSCALE)
                            else:
                                ins = s.copy(out=slot, in_=kpb[idx % 2])
                            sm.tag(ins, "skv")
                    for vg in range(NVG):
                        idx = h * NVG + vg
                        sm.wait(s, "tvv", idx + 1)
                        if idx >= 4:
                            sm.wait(s, "vst%d" % (idx % 4),
                                    16 * ((idx - 4) // 4 + 1))
                        sm.tag(s.copy(out=vb[:, ts(idx % 4, 512)],
                                      in_=vpb[idx % 2]), "svv")
                    sm.mark(s, "sbself")

            @blk.sync
            def _(sync):
                for h in range(HC):
                    for mi in range(2):
                        dsc = (k_sc, q_sc)[mi]
                        for mt in range(NMT):
                            idx = h * 2 * NMT + mi * NMT + mt
                            sm.wait(sync, "skv", idx + 1)
                            sm.tag(sync.dma_start(out=dsc[h][:, ts(mt, 512)],
                                                  in_=kqb[:, ts(idx % 4, 512)]),
                                   "kqst%d" % (idx % 4), 16)
                    for vg in range(NVG):
                        idx = h * NVG + vg
                        sm.wait(sync, "svv", idx + 1)
                        sm.tag(sync.dma_start(out=v_sc[h][:, ts(vg, 512)],
                                              in_=vb[:, ts(idx % 4, 512)]),
                               "vst%d" % (idx % 4), 16)
                for q in range(4):
                    sync.wait_ge(sm.get("kqst%d" % q), sm.val("kqst%d" % q))
                    sync.wait_ge(sm.get("vst%d" % q), sm.val("vst%d" % q))

            @blk.vector
            def _(v):
                sui = sm.n.get("sui_done", 0)
                k = 0
                while sui < NSTAT:
                    sm.wait(v, "v1bself", k)
                    stats_unit(v, sui)
                    sm.mark(v, "v1bself")
                    sui += 1
                    k += 1

    if cfg.stop_after == "b1b":
        return nc, es
    with (
        nc.psum_tensor("psb", [128, 3], F32) as psb_g,
        nc.named_scope("b1c"), nc.Block("b1c") as blk,
    ):
        psb = psb_g[:]

        @blk.vector
        def _(v):
            sm.tag(v.tensor_reduce(out=redc[:, 0:1], in_=s1cols[:, 0:NST1],
                                   axis=AX.X, op=ALU.add), "vqred")
            sm.tag(v.tensor_reduce(out=redc[:, 1:2], in_=s3cols[:, 0:NST1],
                                   axis=AX.X, op=ALU.add), "vqred")
            sm.tag(v.tensor_reduce(out=redc[:, 2:3], in_=s2cols[:, 0:NST1],
                                   axis=AX.X, op=ALU.add), "vqred")

        @blk.tensor
        def _(t):
            sm.wait(t, "vqred", 3)
            sm.tag(t.matmul(psb, ones_f32, redc[:, 0:3], start=True, stop=True),
                   "tred")

        @blk.scalar
        def _(s):
            sm.wait(s, "tred", 1)
            s.mul(out=h1c, in_=psb[:, 0:1], mul=0.5 / NW)
            s.mul(out=h3c, in_=psb[:, 1:2], mul=0.5 / NW)
            s.mul(out=h2c, in_=psb[:, 2:3], mul=0.5 / NW)
            s.mul(out=h1n, in_=psb[:, 0:1], mul=-0.5 / NW)
            s.mul(out=h3n, in_=psb[:, 1:2], mul=-0.5 / NW)
            s.mul(out=h2n, in_=psb[:, 2:3], mul=-0.5 / NW)

    if cfg.stop_after == "b1c":
        return nc, es
    pair_list = []
    for h in range(HC):
        for ti in range(NMT):
            nmc = 4 * ti + 4
            for pr in range(nmc // 2):
                pair_list.append((h, ti, pr, nmc))
    diag_before = []
    dcnt = 0
    for (h, ti, pr, nmc) in pair_list:
        diag_before.append(dcnt)
        if 2 * pr + 1 >= nmc - 4:
            dcnt += 1
    paircum = []
    acc = 0
    for ti in range(NMT):
        acc += (4 * ti + 4) // 2
        paircum.append(acc)

    def pair_cum(h, ti):
        return h * paircum[-1] + paircum[ti]

    with (
        nc.sbuf_tensor("ksb", [128, 2 * S], BF16) as ksb_g,
        nc.sbuf_tensor("qsb", [128, 2 * S], BF16) as qsb_g,
        nc.sbuf_tensor("vaug", [128, 2 * NCK * 129], BF16) as vaug_g,
        nc.sbuf_tensor("p_sb", [128, 2 * 1024], BF16) as p_sb_g,
        nc.sbuf_tensor("ztm", [128, 2 * S], BF16) as ztm_g,
        nc.sbuf_tensor("zfm", [128, 2 * S], BF16, is_transpose_dest=True) as zfm_g,
        nc.sbuf_tensor("recipb", [128, 8], F32) as recip_g,
        nc.psum_tensor("sp0", [128, 1024], F32) as sp0_g,
        nc.psum_tensor("sp1", [128, 1024], F32) as sp1_g,
        nc.psum_tensor("zp0", [128, 512], F32) as zp0_g,
        nc.psum_tensor("zp1", [128, 512], F32) as zp1_g,
        nc.psum_tensor("zp2", [128, 512], F32) as zp2_g,
        nc.psum_tensor("zp3", [128, 512], F32) as zp3_g,
        nc.named_scope("b2"), nc.Block("b2") as blk,
    ):
        ksb, qsb, vaug = ksb_g[:], qsb_g[:], vaug_g[:]
        p_sb, ztm, zfm, recipb = p_sb_g[:], ztm_g[:], zfm_g[:], recip_g[:]
        spb = [sp0_g[:], sp1_g[:]]
        zpb = [zp0_g[:], zp1_g[:], zp2_g[:], zp3_g[:]]

        def vaug_view(slot):
            return vaug[:, slot * NCK * 129:(slot + 1) * NCK * 129].rearrange(
                "p (m r) -> p m r", r=129)

        @blk.sync
        def _(sync):
            for h in range(HC + 1):
                if h < HC:
                    if h >= 2:
                        sm.wait(sync, "pav", pair_cum(h - 2, NMT - 1))
                    sm.tag(sync.dma_start(out=ksb[:, ts(h % 2, S)], in_=k_sc[h]),
                           "akl%d" % (h % 2), 16)
                    sm.tag(sync.dma_start(out=qsb[:, ts(h % 2, S)], in_=q_sc[h]),
                           "akl%d" % (h % 2), 16)
                    sm.tag(sync.dma_start(out=vaug_view(h % 2)[:, :, 0:128],
                                          in_=v_sc[h].rearrange(
                                              "p (m c) -> p m c", c=128)),
                           "akl%d" % (h % 2), 16)
                if h >= 1:
                    hz = h - 1
                    sm.wait(sync, "vdiv", (hz + 1) * NMT * 4)
                    for ci in range(NCK):
                        sm.tag(sync.dma_start_transpose(
                            out=zfm[:, (hz % 2) * S + ci * 128:
                                    (hz % 2) * S + (ci + 1) * 128],
                            in_=ztm[:, (hz % 2) * S + ci * 128:
                                    (hz % 2) * S + (ci + 1) * 128]), "ztr", 16)
                    sm.wait(sync, "ztr", sm.val("ztr"))
                    sm.tag(sync.dma_start(out=ag_in[hz],
                                          in_=zfm[:, ts(hz % 2, S)]),
                           "zst%d" % (hz % 2), 16)
            sync.wait_ge(sm.get("zst0"), sm.val("zst0"))
            sync.wait_ge(sm.get("zst1"), sm.val("zst1"))

        @blk.vector
        def _(v):
            v.memset(vaug_view(0)[:, :, 128:129], 1.0)
            sm.tag(v.memset(vaug_view(1)[:, :, 128:129], 1.0), "vone")
            for pidx, (h, ti, pr, nmc) in enumerate(pair_list):
                diag = (2 * pr + 1 >= nmc - 4)
                if diag:
                    sm.wait(v, "tsc", pidx + 1)
                    last = None
                    for u in range(2):
                        mc = 2 * pr + u
                        dd = mc - (nmc - 4)
                        if dd >= 0:
                            dlt = dd * 128
                            last = v.tensor_tensor(
                                out=spb[pidx % 2][:, u * 512 + dlt:
                                                  u * 512 + dlt + 128],
                                in0=spb[pidx % 2][:, u * 512 + dlt:
                                                  u * 512 + dlt + 128],
                                in1=maskst, op=ALU.add)
                    sm.tag(last, "vmk")
                if pr == nmc // 2 - 1:
                    tidx = h * NMT + ti
                    h2 = h % 2
                    sm.wait(v, "pav", pair_cum(h, ti))
                    if h >= 2:
                        sm.wait(v, "zst%d" % (h % 2), 16 * ((h - 2) // 2 + 1))
                    for tc in range(4):
                        sums = bass.AP(zpb[tc].tensor, zpb[tc].offset + 128,
                                       [list(zpb[tc].ap[0]), [1, 1]])
                        sm.tag(v.reciprocal(out=recipb[:, h2 * 4 + tc:
                                                       h2 * 4 + tc + 1],
                                            in_=sums), "vrc")
                    sm.wait(v, "vrc", sm.val("vrc"))
                    for tc in range(4):
                        ci = ti * 4 + tc
                        sm.tag(v.tensor_scalar(
                            out=ztm[:, h2 * S + ci * 128: h2 * S + (ci + 1) * 128],
                            in0=zpb[tc][:, 0:128],
                            scalar1=recipb[:, h2 * 4 + tc: h2 * 4 + tc + 1],
                            scalar2=None, op0=ALU.mult), "vdiv")

        @blk.scalar
        def _(s):
            for pidx, (h, ti, pr, nmc) in enumerate(pair_list):
                diag = (2 * pr + 1 >= nmc - 4)
                if diag:
                    sm.wait(s, "vmk", diag_before[pidx] + 1)
                else:
                    sm.wait(s, "tsc", pidx + 1)
                if pidx >= 2:
                    sm.wait(s, "pav", pidx - 1)
                full0 = (2 * pr) < (nmc - 4)
                full1 = (2 * pr + 1) < (nmc - 4)
                if full0 and full1:
                    sm.tag(s.activation(out=p_sb[:, ts(pidx % 2, 1024)],
                                        in_=spb[pidx % 2][:, 0:1024],
                                        func=AF.Exp), "sexp")
                else:
                    last = None
                    for u in range(2):
                        mc = 2 * pr + u
                        dd = mc - (nmc - 4)
                        dlt = max(dd, 0) * 128
                        last = s.activation(
                            out=p_sb[:, (pidx % 2) * 1024 + u * 512 + dlt:
                                     (pidx % 2) * 1024 + (u + 1) * 512],
                            in_=spb[pidx % 2][:, u * 512 + dlt:(u + 1) * 512],
                            func=AF.Exp)
                    sm.tag(last, "sexp")

        @blk.tensor
        def _(t):
            def av(pidx):
                h, ti, pr, nmc = pair_list[pidx]
                h2 = h % 2
                if pr == 0:
                    tidx = h * NMT + ti
                    if tidx >= 1:
                        sm.wait(t, "vdiv", tidx * 4)
                sm.wait(t, "sexp", pidx + 1)
                last = None
                for u in range(2):
                    mc = 2 * pr + u
                    dd = mc - (nmc - 4)
                    dlt = max(dd, 0) * 128
                    for tc in range(dlt // 128, 4):
                        last = t.matmul(
                            zpb[tc][:, 0:129],
                            p_sb[:, (pidx % 2) * 1024 + u * 512 + tc * 128:
                                 (pidx % 2) * 1024 + u * 512 + (tc + 1) * 128],
                            vaug_view(h2)[:, mc, :],
                            start=(mc == 0), stop=(mc == nmc - 4 + tc))
                sm.tag(last, "pav")

            for pidx, (h, ti, pr, nmc) in enumerate(pair_list):
                h2 = h % 2
                if pr == 0:
                    if ti == 0:
                        sm.wait(t, "akl%d" % (h % 2), 16 * 3 * (h // 2 + 1))
                        if h == 0:
                            sm.wait(t, "vone", 1)
                if pidx >= 2:
                    sm.wait(t, "sexp", pidx - 1)
                last = None
                for u in range(2):
                    mc = 2 * pr + u
                    dd = mc - (nmc - 4)
                    dlt = max(dd, 0) * 128
                    last = t.matmul(
                        spb[pidx % 2][:, u * 512 + dlt:(u + 1) * 512],
                        ksb[:, h2 * S + mc * 128: h2 * S + (mc + 1) * 128],
                        qsb[:, h2 * S + ti * 512 + dlt: h2 * S + (ti + 1) * 512],
                        start=True, stop=True)
                sm.tag(last, "tsc")
                if pr >= 1:
                    av(pidx - 1)
                if pr == nmc // 2 - 1:
                    av(pidx)

        @blk.gpsimd
        def _(g):
            sm.wait(g, "zst0", sm.val("zst0"))
            sm.wait(g, "zst1", sm.val("zst1"))
            cc = g.collective_compute(
                "AllGather", ALU.bypass,
                replica_groups=[[0, 1], [2, 3], [4, 5], [6, 7]],
                ins=[ag_in], outs=[ag_out])
            sm.tag(cc, "ccd")

    if cfg.stop_after == "b2":
        return nc, es
    with (
        nc.sbuf_tensor("w2b", [128, 2 * J], F32) as w2b_g,
        nc.sbuf_tensor("sg2a", [128, J], BF16) as sg2a_g,
        nc.sbuf_tensor("sg2b", [128, J], BF16) as sg2b_g,
        nc.sbuf_tensor("r2tm", [128, 2 * J], BF16) as r2tm_g,
        nc.sbuf_tensor("r2fmb", [128, 2 * NJ * 128], BF16) as r2fmb_g,
        nc.psum_tensor("pq0", [128, 1024], BF16) as pq0_g,
        nc.psum_tensor("pq1", [128, 1024], BF16) as pq1_g,
        nc.named_scope("b4a"), nc.Block("b4a") as blk,
    ):
        w2b, sg2a, sg2b = w2b_g[:], sg2a_g[:], sg2b_g[:]
        r2tm, r2fmb = r2tm_g[:], r2fmb_g[:]
        pqb = [pq0_g[:], pq1_g[:]]
        QBAT = batches(NJ, 8)
        NQB = len(QBAT)

        @blk.sync
        def _(sync):
            for dc in range(ND):
                if dc >= 2:
                    sm.wait(sync, "sq2", dc - 1)
                sm.tag(sync.dma_start(out=w2b[:, ts(dc % 2, J)],
                                      in_=w2[ts(dc, 128), :]), "w2l%d" % (dc % 2), 16)
                if dc >= 1:
                    sm.wait(sync, "evq", dc * NQB)
                    sm.tag(sync.dma_start(out=r2_sc[dc - 1],
                                          in_=r2fmb[:, ts((dc - 1) % 2, NJ * 128)]),
                           "r2st%d" % ((dc - 1) % 2), 16)
            sm.wait(sync, "evq", ND * NQB)
            sm.tag(sync.dma_start(out=r2_sc[ND - 1],
                                  in_=r2fmb[:, ts((ND - 1) % 2, NJ * 128)]),
                   "r2st%d" % ((ND - 1) % 2), 16)
            sync.wait_ge(sm.get("r2st0"), sm.val("r2st0"))
            sync.wait_ge(sm.get("r2st1"), sm.val("r2st1"))

        @blk.scalar
        def _(s):
            for dc in range(ND):
                sm.wait(s, "sqself", dc)
                sm.wait(s, "w2l%d" % (dc % 2), 16 * (dc // 2 + 1))
                sm.wait(s, "vq2", dc)
                s.activation(out=sg2a, in_=w2b[:, ts(dc % 2, J)], func=AF.Sign,
                             bias=h2n)
                sm.tag(s.activation(out=sg2b, in_=w2b[:, ts(dc % 2, J)],
                                    func=AF.Sign, bias=h2c), "sq2")
                sm.mark(s, "sqself")

        @blk.vector
        def _(v):
            v.memset(r2fmb[:], 0.0)

            def evq(dc):
                for bi, bat in enumerate(QBAT):
                    gq = dc * NQB + bi
                    sm.wait(v, "tq2", gq + 1)
                    if dc >= 2:
                        sm.wait(v, "r2st%d" % (dc % 2), 16 * ((dc - 2) // 2 + 1))
                    lo = bat[0] * 128
                    base = (dc % 2) * NJ * 128
                    if bat[-1] == NJ - 1:
                        hi = bat[-1] * 128
                        if hi > lo:
                            v.tensor_copy(out=r2fmb[:, base + lo: base + hi],
                                          in_=pqb[gq % 2][:, 0:hi - lo])
                        sm.tag(v.tensor_copy(
                            out=r2fmb[0:JLAST, base + hi: base + hi + 128],
                            in_=pqb[gq % 2][0:JLAST, hi - lo: hi - lo + 128]),
                            "evq")
                    else:
                        hi = bat[-1] * 128 + 128
                        sm.tag(v.tensor_copy(out=r2fmb[:, base + lo: base + hi],
                                             in_=pqb[gq % 2][:, 0:hi - lo]), "evq")

            for dc in range(ND):
                sm.wait(v, "vqself", dc)
                sm.wait(v, "sq2", dc + 1)
                if dc >= 2:
                    sm.wait(v, "tq2", (dc - 1) * NQB)
                sm.tag(v.tensor_add(out=r2tm[:, ts(dc % 2, J)], in0=sg2a,
                                    in1=sg2b), "vq2")
                if dc >= 1:
                    evq(dc - 1)
                sm.mark(v, "vqself")
            evq(ND - 1)

        @blk.tensor
        def _(t):
            for dc in range(ND):
                sm.wait(t, "vq2", dc + 1)
                for bi, bat in enumerate(QBAT):
                    gidx = dc * NQB + bi
                    if gidx >= 2:
                        sm.wait(t, "evq", gidx - 1)
                    pb = pqb[gidx % 2]
                    last = None
                    for k, jc in enumerate(bat):
                        rows = 128 if jc < NJ - 1 else JLAST
                        last = t.transpose(pb[0:rows, ts(k, 128)],
                                           r2tm[:, (dc % 2) * J + jc * 128:
                                                (dc % 2) * J + jc * 128 + rows],
                                           ident)
                    sm.tag(last, "tq2")

    if cfg.stop_after == "b4a":
        return nc, es
    with (
        nc.sbuf_tensor("xnow", [128, ND * TOWN], BF16) as xnow_g,
        nc.sbuf_tensor("w13b", [128, 4 * D], F32) as w13b_g,
        nc.sbuf_tensor("sg1a", [128, D], BF16) as sg1a_g,
        nc.sbuf_tensor("sg1b", [128, D], BF16) as sg1b_g,
        nc.sbuf_tensor("r13tm", [128, 4 * D], BF16) as r13tm_g,
        nc.sbuf_tensor("r13fm", [128, 2 * 2 * D], BF16) as r13fm_g,
        nc.sbuf_tensor("shb", [128, 2 * 512], F32) as shb_g,
        nc.sbuf_tensor("gb", [128, 2 * TOWN], BF16) as gb_g,
        nc.psum_tensor("pf0", [128, 1024], BF16) as pf0_g,
        nc.psum_tensor("pf1", [128, 1024], BF16) as pf1_g,
        nc.psum_tensor("h1p0", [128, 512], F32) as h1p0_g,
        nc.psum_tensor("h3p0", [128, 512], F32) as h3p0_g,
        nc.psum_tensor("h1p1", [128, 512], F32) as h1p1_g,
        nc.psum_tensor("h3p1", [128, 512], F32) as h3p1_g,
        nc.named_scope("b4b"), nc.Block("b4b") as blk,
    ):
        xnow, w13b = xnow_g[:], w13b_g[:]
        sg1a, sg1b, r13tm, r13fm = sg1a_g[:], sg1b_g[:], r13tm_g[:], r13fm_g[:]
        shb, gb = shb_g[:], gb_g[:]
        pfb = [pf0_g[:], pf1_g[:]]
        h1p = [h1p0_g[:], h1p1_g[:]]
        h3p = [h3p0_g[:], h3p1_g[:]]
        FBAT = batches(2 * ND, 8)
        NFB = len(FBAT)

        @blk.sync
        def _(sync):
            for c in range(ND):
                sm.tag(sync.dma_start(out=xnow[:, ts(c, TOWN)], in_=xn_own_sc[c]),
                       "xol", 16)
            for jc in range(NJ):
                rows = 128 if jc < NJ - 1 else JLAST
                if jc >= 2:
                    sm.wait(sync, "sgn1", 2 * (jc - 1))
                sm.tag(sync.dma_start(
                    out=w13b[0:rows, (jc % 2) * 2 * D:(jc % 2) * 2 * D + D],
                    in_=w1[jc * 128: jc * 128 + rows, :]), "w13l%d" % (jc % 2), 16)
                sm.tag(sync.dma_start(
                    out=w13b[0:rows, (jc % 2) * 2 * D + D:(jc % 2) * 2 * D + 2 * D],
                    in_=w3[jc * 128: jc * 128 + rows, :]), "w13l%d" % (jc % 2), 16)
                if jc >= 1:
                    pj = jc - 1
                    sm.wait(sync, "gev", (pj + 1) * NTT)
                    sm.tag(sync.dma_start(out=g_sc[pj],
                                          in_=gb[:, ts(pj % 2, TOWN)]),
                           "gst%d" % (pj % 2), 16)
            pj = NJ - 1
            sm.wait(sync, "gev", (pj + 1) * NTT)
            sm.tag(sync.dma_start(out=g_sc[pj],
                                  in_=gb[:, ts(pj % 2, TOWN)]),
                   "gst%d" % (pj % 2), 16)
            sync.wait_ge(sm.get("gst0"), sm.val("gst0"))
            sync.wait_ge(sm.get("gst1"), sm.val("gst1"))

        @blk.scalar
        def _(s):
            for jc in range(NJ):
                rows = 128 if jc < NJ - 1 else JLAST
                sm.wait(s, "sfself", jc)
                sm.wait(s, "w13l%d" % (jc % 2), 32 * (jc // 2 + 1))
                sm.wait(s, "vr1", 2 * jc)
                sl = (jc % 2) * 2 * D
                s.activation(out=sg1a[0:rows, :], in_=w13b[0:rows, sl:sl + D],
                             func=AF.Sign, bias=h1n[0:rows, :])
                sm.tag(s.activation(out=sg1b[0:rows, :],
                                    in_=w13b[0:rows, sl:sl + D],
                                    func=AF.Sign, bias=h1c[0:rows, :]), "sgn1")
                sm.wait(s, "vr1", 2 * jc + 1)
                s.activation(out=sg1a[0:rows, :],
                             in_=w13b[0:rows, sl + D:sl + 2 * D],
                             func=AF.Sign, bias=h3n[0:rows, :])
                sm.tag(s.activation(out=sg1b[0:rows, :],
                                    in_=w13b[0:rows, sl + D:sl + 2 * D],
                                    func=AF.Sign, bias=h3c[0:rows, :]), "sgn1")
                for tt in range(NTT):
                    idx = jc * NTT + tt
                    sm.wait(s, "th1", idx + 1)
                    if idx >= 2:
                        sm.wait(s, "gev", idx - 1)
                    sm.tag(s.activation(out=shb[0:rows, ts(tt % 2, 512)],
                                        in_=h1p[tt % 2][0:rows, :], func=AF.Silu,
                                        scale=h1c[0:rows, :]), "ssil")
                sm.mark(s, "sfself")

        @blk.vector
        def _(v):
            v.memset(gb[:], 0.0)

            def evf(jc):
                rows = 128 if jc < NJ - 1 else JLAST
                for bi, bat in enumerate(FBAT):
                    gf = jc * NFB + bi
                    sm.wait(v, "tr13", gf + 1)
                    if jc >= 2:
                        sm.wait(v, "th3", (jc - 1) * NTT)
                    base = (jc % 2) * 2 * D
                    if rows == 128:
                        lo, hi = bat[0] * 128, bat[-1] * 128 + 128
                        sm.tag(v.tensor_copy(out=r13fm[:, base + lo: base + hi],
                                             in_=pfb[gf % 2][:, 0:hi - lo]), "evf")
                    else:
                        last = None
                        for k, c in enumerate(bat):
                            last = v.tensor_copy(
                                out=r13fm[:, base + c * 128: base + c * 128 + rows],
                                in_=pfb[gf % 2][:, k * 128: k * 128 + rows])
                        sm.tag(last, "evf")

            for jc in range(NJ):
                rows = 128 if jc < NJ - 1 else JLAST
                sm.wait(v, "vfself", jc)
                if jc >= 2:
                    sm.wait(v, "gst%d" % (jc % 2), 16 * ((jc - 2) // 2 + 1))
                sl = (jc % 2) * 2 * D
                for half in range(2):
                    sm.wait(v, "sgn1", 2 * jc + half + 1)
                    if jc >= 1:
                        sm.wait(v, "tr13", jc * NFB)
                    sm.tag(v.tensor_add(out=r13tm[0:rows, sl + half * D:
                                                  sl + (half + 1) * D],
                                        in0=sg1a[0:rows, :], in1=sg1b[0:rows, :]),
                           "vr1")
                evf(jc)
                for tt in range(NTT):
                    idx = jc * NTT + tt
                    sm.wait(v, "ssil", idx + 1)
                    sm.wait(v, "th3", idx + 1)
                    sm.tag(v.scalar_tensor_tensor(
                        out=gb[0:rows, (jc % 2) * TOWN + tt * 512:
                               (jc % 2) * TOWN + (tt + 1) * 512],
                        in0=h3p[tt % 2][0:rows, :], scalar=h3c[0:rows, :],
                        in1=shb[0:rows, ts(tt % 2, 512)],
                        op0=ALU.mult, op1=ALU.mult), "gev")
                sm.mark(v, "vfself")

        @blk.tensor
        def _(t):
            sm.wait(t, "xol", 16 * ND)
            for jc in range(NJ):
                rows = 128 if jc < NJ - 1 else JLAST
                sm.wait(t, "vr1", 2 * jc + 2)
                for bi, bat in enumerate(FBAT):
                    gidx = jc * NFB + bi
                    if gidx >= 2:
                        sm.wait(t, "evf", gidx - 1)
                    pb = pfb[gidx % 2]
                    last = None
                    for k, c in enumerate(bat):
                        last = t.transpose(pb[0:128, k * 128: k * 128 + rows],
                                           r13tm[0:rows,
                                                 (jc % 2) * 2 * D + c * 128:
                                                 (jc % 2) * 2 * D + (c + 1) * 128],
                                           ident[0:rows, 0:rows])
                    sm.tag(last, "tr13")
                sm.wait(t, "evf", jc * NFB + NFB // 2)
                for tt in range(NTT):
                    idx = jc * NTT + tt
                    if idx >= 2:
                        sm.wait(t, "ssil", idx - 1)
                        sm.wait(t, "gev", idx - 1)
                    last = None
                    for c in range(ND):
                        last = t.matmul(h1p[tt % 2][0:rows, :],
                                        r13fm[:, (jc % 2) * 2 * D + c * 128:
                                              (jc % 2) * 2 * D + c * 128 + rows],
                                        xnow[:, c * TOWN + tt * 512:
                                             c * TOWN + tt * 512 + 512],
                                        start=(c == 0), stop=(c == ND - 1))
                    sm.tag(last, "th1")
                    if tt == 0:
                        sm.wait(t, "evf", (jc + 1) * NFB)
                    last = None
                    for c in range(ND):
                        last = t.matmul(h3p[tt % 2][0:rows, :],
                                        r13fm[:, (jc % 2) * 2 * D + D + c * 128:
                                              (jc % 2) * 2 * D + D + c * 128 + rows],
                                        xnow[:, c * TOWN + tt * 512:
                                             c * TOWN + tt * 512 + 512],
                                        start=(c == 0), stop=(c == ND - 1))
                    sm.tag(last, "th3")

    if cfg.stop_after == "b4b":
        return nc, es
    with (
        nc.sbuf_tensor("wonat", [128, 2 * ND * 128], BF16) as wonat_g,
        nc.sbuf_tensor("wot", [128, NH * D], BF16) as wot_g,
        nc.sbuf_tensor("z16", [128, 2 * NH * 128], BF16) as z16_g,
        nc.sbuf_tensor("attnb", [128, 4 * 512], F32) as attnb_g,
        nc.psum_tensor("po0", [128, 1024], BF16) as po0_g,
        nc.psum_tensor("po1", [128, 1024], BF16) as po1_g,
        nc.psum_tensor("ap0", [128, 512], F32) as ap0_g,
        nc.psum_tensor("ap1", [128, 512], F32) as ap1_g,
        nc.psum_tensor("ap2", [128, 512], F32) as ap2_g,
        nc.psum_tensor("ap3", [128, 512], F32) as ap3_g,
        nc.named_scope("b3"), nc.Block("b3") as blk,
    ):
        wonat, wot, z16, attnb = wonat_g[:], wot_g[:], z16_g[:], attnb_g[:]
        pob = [po0_g[:], po1_g[:]]
        apb = [ap0_g[:], ap1_g[:], ap2_g[:], ap3_g[:]]
        OBAT = batches(ND, 8)
        NOB = len(OBAT)

        @blk.gpsimd
        def _(g):
            for gh in range(NH):
                if gh >= 2:
                    sm.wait(g, "wotr", (gh - 1) * NOB)
                sm.tag(g.dma_start(
                    out=wonat[:, (gh % 2) * ND * 128:(gh % 2 + 1) * ND * 128]
                        .rearrange("p (c e) -> p c e", c=ND),
                    in_=wo[gh].rearrange("(c p) e -> p c e", p=128)),
                    "wol%d" % (gh % 2), 16)
            sm.wait(g, "ccd", 1)
            ovs = [g.value_load(soff[0:1, tc:tc + 1]) for tc in range(NTC)]
            for tc in range(NTC):
                if tc >= 2:
                    sm.wait(g, "tao", (tc - 1) * NDS)
                src = ag_out[:, :, bass.ds(ovs[tc], 128)].rearrange("g p c -> p g c")
                sm.tag(g.dma_start(
                    out=z16[:, (tc % 2) * NH * 128:(tc % 2 + 1) * NH * 128]
                        .rearrange("p (g c) -> p g c", g=NH),
                    in_=src), "z16l%d" % (tc % 2), 16)

        @blk.tensor
        def _(t):
            for gh in range(NH):
                sm.wait(t, "wol%d" % (gh % 2), 16 * (gh // 2 + 1))
                for bi, bat in enumerate(OBAT):
                    gidx = gh * NOB + bi
                    if gidx >= 2:
                        sm.wait(t, "woev", gidx - 1)
                    pb = pob[gidx % 2]
                    last = None
                    for k, c in enumerate(bat):
                        last = t.transpose(
                            pb[:, ts(k, 128)],
                            wonat[:, (gh % 2) * ND * 128 + c * 128:
                                  (gh % 2) * ND * 128 + (c + 1) * 128],
                            ident)
                    sm.tag(last, "wotr")
            sm.wait(t, "woev", NH * NOB)
            for tc in range(NTC):
                sm.wait(t, "z16l%d" % (tc % 2), 16 * (tc // 2 + 1))
                for ds_ in range(NDS):
                    idx = tc * NDS + ds_
                    if idx >= 4:
                        sm.wait(t, "aoev", idx - 3)
                    pb = apb[idx % 4]
                    last = None
                    for gh in range(NH):
                        last = t.matmul(pb,
                                        z16[:, (tc % 2) * NH * 128 + gh * 128:
                                            (tc % 2) * NH * 128 + (gh + 1) * 128],
                                        wot[:, gh * D + ds_ * 512:
                                            gh * D + ds_ * 512 + 512],
                                        start=(gh == 0), stop=(gh == NH - 1))
                    sm.tag(last, "tao")

        @blk.scalar
        def _(s):
            for gh in range(NH):
                for bi, bat in enumerate(OBAT):
                    gidx = gh * NOB + bi
                    sm.wait(s, "wotr", gidx + 1)
                    ncols = len(bat) * 128
                    sm.tag(s.copy(out=wot[:, gh * D + bat[0] * 128:
                                          gh * D + bat[0] * 128 + ncols],
                                  in_=pob[gidx % 2][:, 0:ncols]), "woev")

        @blk.vector
        def _(v):
            for tc in range(NTC):
                for ds_ in range(NDS):
                    idx = tc * NDS + ds_
                    sm.wait(v, "tao", idx + 1)
                    if idx >= 4:
                        sm.wait(v, "aost%d" % (idx % 4), 16 * ((idx - 4) // 4 + 1))
                    sm.tag(v.tensor_copy(out=attnb[:, ts(idx % 4, 512)],
                                         in_=apb[idx % 4]), "aoev")

        @blk.sync
        def _(sync):
            for tc in range(NTC):
                for ds_ in range(NDS):
                    idx = tc * NDS + ds_
                    sm.wait(sync, "aoev", idx + 1)
                    sm.tag(sync.dma_start(out=attn_sc[ts(tc, 128), ts(ds_, 512)],
                                          in_=attnb[:, ts(idx % 4, 512)]),
                           "aost%d" % (idx % 4), 16)
            for q in range(4):
                sync.wait_ge(sm.get("aost%d" % q), sm.val("aost%d" % q))

    if cfg.stop_after == "b3":
        return nc, es
    NTG = TOWN // 256
    with (
        nc.sbuf_tensor("r2sb", [128, NDH * NJ * 128], BF16) as r2sb_g,
        nc.sbuf_tensor("g2sb", [128, 2 * NJ * 256], BF16) as g2sb_g,
        nc.sbuf_tensor("axb", [128, 4 * NDH * 128], F32) as axb_g,
        nc.sbuf_tensor("outb", [128, 2 * NDH * 128], F32) as outb_g,
        nc.psum_tensor("fo0", [128, 512], F32) as fo0_g,
        nc.psum_tensor("fo1", [128, 512], F32) as fo1_g,
        nc.psum_tensor("fo2", [128, 512], F32) as fo2_g,
        nc.psum_tensor("fo3", [128, 512], F32) as fo3_g,
        nc.psum_tensor("fo4", [128, 512], F32) as fo4_g,
        nc.psum_tensor("fo5", [128, 512], F32) as fo5_g,
        nc.psum_tensor("fo6", [128, 512], F32) as fo6_g,
        nc.psum_tensor("fo7", [128, 512], F32) as fo7_g,
        nc.named_scope("b5"), nc.Block("b5") as blk,
    ):
        r2sb, g2sb, axb, outb = r2sb_g[:], g2sb_g[:], axb_g[:], outb_g[:]
        fob = [fo0_g[:], fo1_g[:], fo2_g[:], fo3_g[:],
               fo4_g[:], fo5_g[:], fo6_g[:], fo7_g[:]]
        HW = NDH * 128

        @blk.sync
        def _(sync):
            for dh in range(2):
                if dh == 1:
                    sm.wait(sync, "tfo", NTG)
                for k in range(NDH):
                    if "r2" in cfg.b5_skip:
                        break
                    sm.tag(sync.dma_start(out=r2sb[:, ts(k, NJ * 128)],
                                          in_=r2_sc[dh * NDH + k]), "r2l", 16)
                for tg in range(NTG):
                    gidx = dh * NTG + tg
                    if gidx >= 2:
                        sm.wait(sync, "tfo", gidx - 1)
                    for jb in batches(NJ, 6):
                        if "g2" in cfg.b5_skip:
                            break
                        j0, jn = jb[0], len(jb)
                        src = g_sc[j0:j0 + jn, :, ts(tg, 256)].rearrange(
                            "j p c -> p j c")
                        sm.tag(sync.dma_start(
                            out=g2sb[:, (gidx % 2) * NJ * 256 + j0 * 256:
                                     (gidx % 2) * NJ * 256 + (j0 + jn) * 256]
                                .rearrange("p (j c) -> p j c", j=jn),
                            in_=src), "g2l%d" % (gidx % 2), 16)
                    for u in range(2):
                        tc = 2 * tg + u
                        gi = dh * NTC + tc
                        if gi >= 2:
                            sm.wait(sync, "foev", (gi - 1) * NDSS)
                        if "ax" not in cfg.b5_skip:
                            sm.tag(sync.dma_start(
                                out=axb[:, (gi % 2) * 2 * HW:
                                        (gi % 2) * 2 * HW + HW],
                                in_=attn_sc[ts(tc, 128), dh * HW:(dh + 1) * HW]),
                                "axl%d" % (gi % 2), 16)
                            sm.tag(sync.dma_start(
                                out=axb[:, (gi % 2) * 2 * HW + HW:
                                        (gi % 2 + 1) * 2 * HW],
                                in_=xo[ts(tc, 128), dh * HW:(dh + 1) * HW]),
                                "axl%d" % (gi % 2), 16)
                        if gi >= 1:
                            pgi = gi - 1
                            sm.wait(sync, "foev", (pgi + 1) * NDSS)
                            sm.tag(sync.dma_start(
                                out=out[ts(pgi % NTC, 128),
                                        (pgi // NTC) * HW:(pgi // NTC + 1) * HW],
                                in_=outb[:, ts(pgi % 2, HW)]),
                                "fst%d" % (pgi % 2), 16)
            pgi = 2 * NTC - 1
            sm.wait(sync, "foev", (pgi + 1) * NDSS)
            sm.tag(sync.dma_start(
                out=out[ts(pgi % NTC, 128),
                        (pgi // NTC) * HW:(pgi // NTC + 1) * HW],
                in_=outb[:, ts(pgi % 2, HW)]), "fst%d" % (pgi % 2), 16)
            sync.wait_ge(sm.get("fst0"), sm.val("fst0"))
            sync.wait_ge(sm.get("fst1"), sm.val("fst1"))

        @blk.tensor
        def _(t):
            r2v = r2sb.rearrange("p (k x) -> p k x", k=NDH)
            for dh in range(2):
                if "r2" not in cfg.b5_skip:
                    sm.wait(t, "r2l", 16 * NDH * (dh + 1))
                for tg in range(NTG):
                    gidx = dh * NTG + tg
                    if "g2" not in cfg.b5_skip:
                        sm.wait(t, "g2l%d" % (gidx % 2),
                                16 * len(batches(NJ, 6)) * (gidx // 2 + 1))
                    grp = gidx % 2
                    bank0 = (tg % 2) * 2 * NDSS
                    if gidx >= 2:
                        sm.wait(t, "foev", (gidx - 1) * 2 * NDSS)
                    if "mm" in cfg.b5_skip:
                        sm.mark(t, "tfo")
                    else:
                        last = None
                        for jc in range(NJ):
                            rows = 128 if jc < NJ - 1 else JLAST
                            for u in range(2):
                                for dss in range(NDSS):
                                    rhs = r2v[:, dss * 4:(dss + 1) * 4,
                                              jc * 128: jc * 128 + 128]
                                    last = t.matmul(
                                        fob[bank0 + u * NDSS + dss],
                                        g2sb[0:rows,
                                             grp * NJ * 256 + jc * 256 + u * 128:
                                             grp * NJ * 256 + jc * 256 + (u + 1) * 128],
                                        rhs[0:rows, :, :],
                                        start=(jc == 0), stop=(jc == NJ - 1))
                        sm.tag(last, "tfo")

        @blk.vector
        def _(v):
            for dh in range(2):
                for tg in range(NTG):
                    gidx = dh * NTG + tg
                    sm.wait(v, "tfo", gidx + 1)
                    bank0 = (tg % 2) * 2 * NDSS
                    for u in range(2):
                        gi = dh * NTC + tg * 2 + u
                        if "ax" not in cfg.b5_skip:
                            sm.wait(v, "axl%d" % (gi % 2), 32 * (gi // 2 + 1))
                        if gi >= 2:
                            sm.wait(v, "fst%d" % (gi % 2), 16 * ((gi - 2) // 2 + 1))
                        for dss in range(NDSS):
                            o = outb[:, (gi % 2) * HW + dss * 512:
                                     (gi % 2) * HW + (dss + 1) * 512]
                            if "vec" in cfg.b5_skip:
                                v.memset(o, 0.0)
                                sm.mark(v, "foev")
                                continue
                            v.scalar_tensor_tensor(
                                out=o, in0=fob[bank0 + u * NDSS + dss], scalar=h2c,
                                in1=axb[:, (gi % 2) * 2 * HW + dss * 512:
                                        (gi % 2) * 2 * HW + (dss + 1) * 512],
                                op0=ALU.mult, op1=ALU.add)
                            sm.tag(v.tensor_add(
                                out=o, in0=o,
                                in1=axb[:, (gi % 2) * 2 * HW + HW + dss * 512:
                                        (gi % 2) * 2 * HW + HW + (dss + 1) * 512]),
                                "foev")

    return nc, es


def kernel(L0_attn_x, L0_attn_W_Q, L0_attn_W_K, L0_attn_W_V, L0_attn_W_O,
           L0_ffn_w1, L0_ffn_w3, L0_ffn_w2):
    cfg = Cfg()
    x = np.ascontiguousarray(np.asarray(L0_attn_x, dtype=np.float32))
    WQ = np.ascontiguousarray(np.asarray(L0_attn_W_Q, dtype=np.float32))
    WK = np.ascontiguousarray(np.asarray(L0_attn_W_K, dtype=np.float32))
    WV = np.ascontiguousarray(np.asarray(L0_attn_W_V, dtype=np.float32))
    WO = np.ascontiguousarray(np.asarray(L0_attn_W_O, dtype=np.float32))
    w1 = np.ascontiguousarray(np.asarray(L0_ffn_w1, dtype=np.float32))
    w3 = np.ascontiguousarray(np.asarray(L0_ffn_w3, dtype=np.float32))
    w2 = np.ascontiguousarray(np.asarray(L0_ffn_w2, dtype=np.float32))

    nc, es = build(cfg)
    in_maps = []
    for c in range(8):
        b, hh = c // 2, c % 2
        offs = np.arange(cfg.NTC, dtype=np.int32) * 128 + hh * cfg.TOWN
        in_maps.append({
            "xk": x[b],
            "xo": np.ascontiguousarray(x[b, hh * cfg.TOWN:(hh + 1) * cfg.TOWN]),
            "wq": np.ascontiguousarray(WQ[hh * cfg.HC:(hh + 1) * cfg.HC]),
            "wk": np.ascontiguousarray(WK[hh * cfg.HC:(hh + 1) * cfg.HC]),
            "wv": np.ascontiguousarray(WV[hh * cfg.HC:(hh + 1) * cfg.HC]),
            "wo": WO,
            "w1": w1, "w3": w3, "w2": w2,
            "roff": offs.reshape(1, cfg.NTC),
        })
    res = run_bass_kernel_spmd(nc, in_maps, core_ids=list(range(8)))
    es.close()
    outp = np.empty((4, cfg.S, cfg.D), dtype=np.float32)
    for c in range(8):
        b, hh = c // 2, c % 2
        outp[b, hh * cfg.TOWN:(hh + 1) * cfg.TOWN] = res.results[c]["out"]
    return outp


# version 09 (final)
# speedup vs baseline: 1.0564x; 1.0484x over previous
import math
import numpy as np
import ml_dtypes
from contextlib import ExitStack

import concourse.bass as bass
import concourse.mybir as mybir
from concourse.bass_utils import run_bass_kernel_spmd

F32 = mybir.dt.float32
BF16 = mybir.dt.bfloat16
I32 = mybir.dt.int32
AF = mybir.ActivationFunctionType
ALU = mybir.AluOpType
AX = mybir.AxisListType

RSCALE = 1.0 / math.sqrt(128.0)


def ts(i, n):
    return bass.ts(i, n)


class Cfg:
    def __init__(self, D=2048, S=4096, HC=8, J=5461, NH=16, stop_after=None,
                 b5_skip=(), debug_out=False):
        self.stop_after = stop_after
        self.b5_skip = set(b5_skip)
        self.debug_out = debug_out
        self.D, self.S, self.HC, self.J, self.NH = D, S, HC, J, NH
        self.E = 128
        self.ND = D // 128
        self.NMT = S // 512
        self.NCK = S // 128
        self.TOWN = S // 2
        self.NTC = self.TOWN // 128
        self.NTT = self.TOWN // 512
        self.NJ = (J + 127) // 128
        self.JLAST = J - (self.NJ - 1) * 128
        self.NDH = self.ND // 2
        self.NDSS = (self.NDH * 128) // 512
        assert D % 1024 == 0 and S % 512 == 0 and NH == 2 * HC


class Sems:
    def __init__(self, nc, es):
        self.nc, self.es, self.h, self.n = nc, es, {}, {}

    def get(self, name):
        if name not in self.h:
            self.h[name] = self.es.enter_context(self.nc.semaphore(name))
            self.n[name] = 0
        return self.h[name]

    def tag(self, instr, name, k=1):
        instr.then_inc(self.get(name), k)
        self.n[name] += k
        return self.n[name]

    def mark(self, eng, name, k=1):
        return self.tag(eng.drain(), name, k)

    def val(self, name):
        self.get(name)
        return self.n[name]

    def wait(self, eng, name, v):
        if v > 0:
            eng.wait_ge(self.get(name), v)


def batches(n, size=8):
    out, i = [], 0
    while i < n:
        out.append(list(range(i, min(i + size, n))))
        i += size
    return out


def build(cfg: Cfg):
    nc = bass.Bass("TRN2", target_bir_lowering=False, detect_race_conditions=False)
    D, S, HC, E, ND = cfg.D, cfg.S, cfg.HC, cfg.E, cfg.ND
    NMT, NCK, TOWN, NTC, NTT = cfg.NMT, cfg.NCK, cfg.TOWN, cfg.NTC, cfg.NTT
    J, NJ, JLAST, NH = cfg.J, cfg.NJ, cfg.JLAST, cfg.NH
    NDH, NDSS = cfg.NDH, cfg.NDSS
    NW = float(J * D)
    NDS = D // 512

    xk = nc.dram_tensor("xk", [S, D], F32, kind="ExternalInput").ap()
    xo = nc.dram_tensor("xo", [TOWN, D], F32, kind="ExternalInput").ap()
    wq = nc.dram_tensor("wq", [HC, E, D], F32, kind="ExternalInput").ap()
    wk = nc.dram_tensor("wk", [HC, E, D], F32, kind="ExternalInput").ap()
    wv = nc.dram_tensor("wv", [HC, E, D], F32, kind="ExternalInput").ap()
    wo = nc.dram_tensor("wo", [NH, D, E], F32, kind="ExternalInput").ap()
    w1 = nc.dram_tensor("w1", [J, D], F32, kind="ExternalInput").ap()
    w3 = nc.dram_tensor("w3", [J, D], F32, kind="ExternalInput").ap()
    w2 = nc.dram_tensor("w2", [D, J], F32, kind="ExternalInput").ap()
    roff = nc.dram_tensor("roff", [1, NTC], I32, kind="ExternalInput").ap()
    out = nc.dram_tensor("out", [TOWN, D], F32, kind="ExternalOutput").ap()

    dk = "ExternalOutput" if cfg.debug_out else "Internal"
    xn_dbg = (nc.dram_tensor("xn_dbg", [NCK + TOWN // 128, 128, D], BF16,
                             kind="ExternalOutput").ap() if cfg.debug_out else None)
    ln_dbg = (nc.dram_tensor("ln_dbg", [NCK + TOWN // 128, 128, 18], F32,
                             kind="ExternalOutput").ap() if cfg.debug_out else None)
    xn_own_sc = nc.dram_tensor("xn_own_sc", [ND, 128, TOWN], BF16, kind=dk).ap()
    k_sc = nc.dram_tensor("k_sc", [HC, E, S], BF16, kind=dk).ap()
    q_sc = nc.dram_tensor("q_sc", [HC, E, S], BF16, kind=dk).ap()
    v_sc = nc.dram_tensor("v_sc", [HC, 128, S], BF16, kind=dk).ap()
    ag_in = nc.dram_tensor("ag_in", [HC, E, S], BF16, kind="Internal").ap()
    ag_out = nc.dram_tensor("ag_out", [NH, E, S], BF16, kind="Internal").ap()
    g_sc = nc.dram_tensor("g_sc", [NJ, 128, TOWN], BF16, kind=dk).ap()
    r2_sc = nc.dram_tensor("r2_sc", [ND, 128, NJ * 128], BF16, kind=dk).ap()
    attn_sc = nc.dram_tensor("attn_sc", [TOWN, D], F32, kind=dk).ap()

    es = ExitStack()
    sm = Sems(nc, es)

    ident = nc.alloc_sbuf_tensor("ident", [128, 128], BF16).ap()
    maskst = nc.alloc_sbuf_tensor("maskst", [128, 128], F32).ap()
    ones_f32 = nc.alloc_sbuf_tensor("ones_f32", [128, 128], F32).ap()
    soff = nc.alloc_sbuf_tensor("soff", [1, NTC], I32).ap()

    SCH = 128 * 1024
    NWI = J * D
    NST1 = (NWI + SCH - 1) // SCH
    stats_plan = ([("w1", j) for j in range(NST1)] + [("w3", j) for j in range(NST1)]
                  + [("w2", j) for j in range(NST1)])
    NSTAT = len(stats_plan)
    s1cols = nc.alloc_sbuf_tensor("s1cols", [128, NST1], F32).ap()
    s3cols = nc.alloc_sbuf_tensor("s3cols", [128, NST1], F32).ap()
    s2cols = nc.alloc_sbuf_tensor("s2cols", [128, NST1], F32).ap()
    h1c = nc.alloc_sbuf_tensor("h1c", [128, 1], F32).ap()
    h3c = nc.alloc_sbuf_tensor("h3c", [128, 1], F32).ap()
    h2c = nc.alloc_sbuf_tensor("h2c", [128, 1], F32).ap()
    h1n = nc.alloc_sbuf_tensor("h1n", [128, 1], F32).ap()
    h3n = nc.alloc_sbuf_tensor("h3n", [128, 1], F32).ap()
    h2n = nc.alloc_sbuf_tensor("h2n", [128, 1], F32).ap()
    redc = nc.alloc_sbuf_tensor("redc", [128, 3], F32).ap()
    epsc = nc.alloc_sbuf_tensor("epsc", [128, 1], F32).ap()

    def stats_src(idx):
        kind, j = stats_plan[idx]
        t = {"w1": w1, "w3": w3, "w2": w2}[kind]
        flat = t.rearrange("a b -> (a b)")
        lo = j * SCH
        n = min(SCH, NWI - lo)
        p = n // 1024
        rem = n - p * 1024
        main = flat[lo: lo + p * 1024].rearrange("(p c) -> p c", p=p) if p else None
        tail = (flat[lo + p * 1024: lo + n].rearrange("(p c) -> p c", p=1)
                if rem else None)
        return main, tail, p

    def stats_col(idx):
        kind, j = stats_plan[idx]
        return {"w1": s1cols, "w3": s3cols, "w2": s2cols}[kind][:, j:j + 1]

    with nc.named_scope("b0"), nc.Block("b0") as blk:
        @blk.gpsimd
        def _(g):
            g.dma_start(out=soff[:], in_=roff).then_inc(sm.get("gld0"), 16)
            sm.tag(g.memset(ident[:], 1.0), "g0")
            g.wait_ge(sm.get("g0"), 1)
            sm.tag(g.affine_select(out=ident[:], in_=ident[:], compare_op=ALU.is_ge,
                                   fill=0.0, base=0, pattern=[[1, 128]],
                                   channel_multiplier=-1), "g0")
            g.wait_ge(sm.get("g0"), 2)
            sm.tag(g.affine_select(out=ident[:], in_=ident[:], compare_op=ALU.is_ge,
                                   fill=0.0, base=0, pattern=[[-1, 128]],
                                   channel_multiplier=1), "g0")
            sm.tag(g.memset(maskst[:], 0.0), "g0")
            g.wait_ge(sm.get("g0"), 4)
            g.affine_select(out=maskst[:], in_=maskst[:], compare_op=ALU.is_ge,
                            fill=-1e9, base=0, pattern=[[1, 128]],
                            channel_multiplier=-1)
            g.wait_ge(sm.get("gld0"), 16)

        @blk.vector
        def _(v):
            v.memset(ones_f32[:], 1.0)
            v.memset(epsc[:], 1e-5)
            v.memset(s1cols[:], 0.0)
            v.memset(s3cols[:], 0.0)
            v.memset(s2cols[:], 0.0)

    with (
        nc.sbuf_tensor("xn_fm", [128, ND * S], BF16) as xn_fm_g,
        nc.sbuf_tensor("statw", [128, 2 * 1024], F32) as statw_g,
    ):
        xn_fm = xn_fm_g[:]
        statw = statw_g[:]
        stld_at = []
        _cnt = {0: 0, 1: 0}
        for _sdi in range(NSTAT):
            _m, _t, _p = stats_src(_sdi)
            _cnt[_sdi % 2] += 16 * ((1 if _m is not None else 0)
                                    + (1 if _t is not None else 0))
            stld_at.append(("stld%d" % (_sdi % 2), _cnt[_sdi % 2]))

        def stats_dma(g, sdi):
            if sdi >= 2:
                sm.wait(g, "stdone", sdi - 1)
            main, tail, p = stats_src(sdi)
            sem = "stld%d" % (sdi % 2)
            if main is not None:
                sm.tag(g.dma_start(out=statw[0:p, ts(sdi % 2, 1024)], in_=main),
                       sem, 16)
            if tail is not None:
                sm.tag(g.dma_start(out=statw[p:p + 1, (sdi % 2) * 1024:
                                             (sdi % 2) * 1024 + tail.shape[1]],
                                   in_=tail), sem, 16)

        def stats_unit(v, sui):
            sm.wait(v, stld_at[sui][0], stld_at[sui][1])
            main, tail, p = stats_src(sui)
            rows = p + (1 if tail is not None else 0)
            sm.tag(v.tensor_reduce(out=stats_col(sui)[0:rows, :],
                                   in_=statw[0:rows, ts(sui % 2, 1024)],
                                   axis=AX.X, op=ALU.add,
                                   apply_absolute_value=True), "stdone")

        nchunks = NCK + NTC
        with (
            nc.sbuf_tensor("x_sb", [128, 3 * D], F32) as x_sb_g,
            nc.sbuf_tensor("trash", [128, D], BF16) as trash_g,
            nc.sbuf_tensor("xn_tm", [128, 2 * D], BF16) as xn_tm_g,
            nc.sbuf_tensor("obuf", [128, 2 * D], BF16) as obuf_g,
            nc.sbuf_tensor("lncol", [128, 18], F32) as lncol_g,
            nc.psum_tensor("pt0", [128, 1024], BF16) as pt0_g,
            nc.psum_tensor("pt1", [128, 1024], BF16) as pt1_g,
            nc.named_scope("b1a"), nc.Block("b1a") as blk,
        ):
            x_sb, trash = x_sb_g[:], trash_g[:]
            xn_tm, obuf, lncol = xn_tm_g[:], obuf_g[:], lncol_g[:]
            ptb = [pt0_g[:], pt1_g[:]]

            @blk.sync
            def _(sync):
                for i in range(nchunks):
                    if cfg.debug_out and i >= 1:
                        sm.wait(sync, "snm", i)
                        sm.tag(sync.dma_start(out=xn_dbg[i - 1],
                                              in_=xn_tm[:, ts((i - 1) % 2, D)]),
                               "xdbg", 16)
                        sm.tag(sync.dma_start(out=ln_dbg[i - 1], in_=lncol[:]),
                               "xdbg", 16)
                        sm.wait(sync, "xdbg", sm.val("xdbg"))
                    if i >= 3:
                        sm.wait(sync, "vself", i - 2)
                        sm.wait(sync, "sself", i - 2)
                    src = (xk[ts(i, 128), :] if i < NCK else xo[ts(i - NCK, 128), :])
                    sm.tag(sync.dma_start(out=x_sb[:, ts(i % 3, D)], in_=src),
                           "xld%d" % (i % 3), 16)
                    if i >= NCK + 1:
                        oj = i - NCK - 1
                        sm.wait(sync, "tev", (NCK + oj + 1) * ((ND + 7) // 8))
                        sm.tag(sync.dma_start(
                            out=xn_own_sc[:, :, ts(oj, 128)].rearrange(
                                "c p t -> p c t"),
                            in_=obuf[:, ts(oj % 2, D)].rearrange(
                                "p (c e) -> p c e", c=ND)), "ost%d" % (oj % 2), 16)
                if cfg.debug_out:
                    sm.wait(sync, "snm", nchunks)
                    sm.tag(sync.dma_start(out=xn_dbg[nchunks - 1],
                                          in_=xn_tm[:, ts((nchunks - 1) % 2, D)]),
                           "xdbg", 16)
                    sm.tag(sync.dma_start(out=ln_dbg[nchunks - 1], in_=lncol[:]),
                           "xdbg", 16)
                    sm.wait(sync, "xdbg", sm.val("xdbg"))
                oj = NTC - 1
                sm.wait(sync, "tev", (NCK + oj + 1) * ((ND + 7) // 8))
                sm.tag(sync.dma_start(
                    out=xn_own_sc[:, :, ts(oj, 128)].rearrange("c p t -> p c t"),
                    in_=obuf[:, ts(oj % 2, D)].rearrange("p (c e) -> p c e", c=ND)),
                    "ost%d" % (oj % 2), 16)
                sync.wait_ge(sm.get("ost0"), sm.val("ost0"))
                sync.wait_ge(sm.get("ost1"), sm.val("ost1"))

            TBAT = batches(ND, 8)
            NTB = len(TBAT)

            def evac(v, ci):
                if ci >= NCK:
                    oi = ci - NCK
                    if oi >= 2:
                        sm.wait(v, "ost%d" % (oi % 2), 16 * ((oi - 2) // 2 + 1))
                for bi, bat in enumerate(TBAT):
                    gidx = ci * NTB + bi
                    sm.wait(v, "ttr", gidx + 1)
                    pb = ptb[gidx % 2]
                    nb = len(bat)
                    if ci < NCK:
                        dst = xn_fm.rearrange("p (c s) -> p c s", c=ND)[
                            :, bat[0]:bat[0] + nb, ts(ci, 128)]
                    else:
                        oi = ci - NCK
                        dst = obuf[:, ts(oi % 2, D)].rearrange(
                            "p (c e) -> p c e", c=ND)[:, bat[0]:bat[0] + nb, :]
                    sm.tag(v.tensor_copy(out=dst,
                                         in_=pb[:, 0:nb * 128].rearrange(
                                             "p (c e) -> p c e", c=nb)), "tev")

            @blk.vector
            def _(v):
                sui = 0
                for i in range(nchunks):
                    sm.wait(v, "vself", i)
                    sl = i % 3
                    sm.wait(v, "xld%d" % sl, 16 * (i // 3 + 1))
                    c_s = lncol[:, sl:sl + 1]
                    c_m = lncol[:, 3 + sl:4 + sl]
                    c_v = lncol[:, 6 + sl:7 + sl]
                    c_r = lncol[:, 9 + sl:10 + sl]
                    c_b = lncol[:, 12 + sl:13 + sl]
                    c_q = lncol[:, 15 + sl:16 + sl]
                    c1 = sm.tag(v.reduce_sum(out=c_s, in_=x_sb[:, ts(sl, D)],
                                             axis=AX.X), "vrd")
                    sm.wait(v, "vrd", c1)
                    c2 = sm.tag(v.tensor_scalar(out=c_m, in0=c_s, scalar1=1.0 / D,
                                                scalar2=None, op0=ALU.mult), "vrd")
                    sm.wait(v, "ssq", i + 1)
                    sm.wait(v, "vrd", c2)
                    c3 = sm.tag(v.tensor_mul(out=c_v, in0=c_m, in1=c_m), "vrd")
                    sm.wait(v, "vrd", c3)
                    sm.tag(v.scalar_tensor_tensor(out=c_v, in0=c_q, scalar=1.0 / D,
                                                  in1=c_v, op0=ALU.mult,
                                                  op1=ALU.subtract), "vvar")
                    sm.wait(v, "srs", i + 1)
                    sm.tag(v.scalar_tensor_tensor(out=c_b, in0=c_m, scalar=-1.0,
                                                  in1=c_r, op0=ALU.mult,
                                                  op1=ALU.mult), "vnb")
                    if i >= 1:
                        evac(v, i - 1)
                    if sui < NSTAT and i % 2 == 0:
                        stats_unit(v, sui)
                        sui += 1
                    sm.mark(v, "vself")
                evac(v, nchunks - 1)
                sm.n["sui_done"] = sui

            @blk.scalar
            def _(s):
                for i in range(nchunks):
                    sm.wait(s, "sself", i)
                    sl = i % 3
                    sm.wait(s, "xld%d" % sl, 16 * (i // 3 + 1))
                    c_v = lncol[:, 6 + sl:7 + sl]
                    c_r = lncol[:, 9 + sl:10 + sl]
                    c_q = lncol[:, 15 + sl:16 + sl]
                    sm.tag(s.activation(out=trash[:], in_=x_sb[:, ts(sl, D)],
                                        func=AF.Square, accum_out=c_q), "ssq")
                    sm.wait(s, "vvar", i + 1)
                    s.activation(out=c_r, in_=c_v, func=AF.Ln, bias=epsc)
                    sm.tag(s.activation(out=c_r, in_=c_r, func=AF.Exp, scale=-0.5),
                           "srs")
                    sm.wait(s, "vnb", i + 1)
                    if i >= 2:
                        sm.wait(s, "ttr", (i - 1) * ((ND + 7) // 8))
                    sm.tag(s.activation(out=xn_tm[:, ts(i % 2, D)],
                                        in_=x_sb[:, ts(sl, D)], func=AF.Identity,
                                        bias=lncol[:, 12 + sl:13 + sl], scale=c_r),
                           "snm")
                    sm.mark(s, "sself")

            @blk.tensor
            def _(t):
                for i in range(nchunks):
                    sm.wait(t, "snm", i + 1)
                    for bi, bat in enumerate(TBAT):
                        gidx = i * NTB + bi
                        if gidx >= 2:
                            sm.wait(t, "tev", gidx - 1)
                        pb = ptb[gidx % 2]
                        last = None
                        for k, c in enumerate(bat):
                            last = t.transpose(pb[:, ts(k, 128)],
                                               xn_tm[:, (i % 2) * D + c * 128:
                                                     (i % 2) * D + (c + 1) * 128],
                                               ident)
                        sm.tag(last, "ttr")

            @blk.gpsimd
            def _(g):
                nb1a = min((nchunks + 1) // 2, NSTAT)
                for sdi in range(min(nb1a + 2, NSTAT)):
                    stats_dma(g, sdi)
                sm.n["sdi_done"] = min(nb1a + 2, NSTAT)

        if cfg.stop_after == "b1a":
            return nc, es
        with (
            nc.sbuf_tensor("wnat", [128, 2 * 3 * D], BF16) as wnat_g,
            nc.sbuf_tensor("wT", [128, 3 * D], BF16) as wT_g,
            nc.sbuf_tensor("kqb", [128, 4 * 512], BF16) as kqb_g,
            nc.sbuf_tensor("vb", [128, 4 * 512], BF16) as vb_g,
            nc.psum_tensor("pw0", [128, 1024], BF16) as pw0_g,
            nc.psum_tensor("pw1", [128, 1024], BF16) as pw1_g,
            nc.psum_tensor("kp0", [128, 512], F32) as kp0_g,
            nc.psum_tensor("kp1", [128, 512], F32) as kp1_g,
            nc.psum_tensor("vp0", [128, 512], F32) as vp0_g,
            nc.psum_tensor("vp1", [128, 512], F32) as vp1_g,
            nc.named_scope("b1b"), nc.Block("b1b") as blk,
        ):
            wnat, wT, kqb, vb = wnat_g[:], wT_g[:], kqb_g[:], vb_g[:]
            pwb = [pw0_g[:], pw1_g[:]]
            kpb = [kp0_g[:], kp1_g[:]]
            vpb = [vp0_g[:], vp1_g[:]]
            WBAT = batches(3 * ND, 8)
            NWB = len(WBAT)
            NVG = NCK // 4

            @blk.gpsimd
            def _(g):
                sdi = sm.n.get("sdi_done", NSTAT)
                for h in range(HC):
                    if h >= 2:
                        sm.wait(g, "twtr", (h - 1) * NWB)
                    for mi, wsrc in enumerate((wq, wk, wv)):
                        sm.tag(g.dma_start(
                            out=wnat[:, (h % 2) * 3 * D + mi * D:
                                     (h % 2) * 3 * D + (mi + 1) * D],
                            in_=wsrc[h]), "gwl%d" % (h % 2), 16)
                    quota = (NSTAT - sm.n.get("sdi_done", NSTAT) + HC - 1) // HC
                    for _ in range(quota):
                        if sdi < NSTAT:
                            stats_dma(g, sdi)
                            sdi += 1
                while sdi < NSTAT:
                    stats_dma(g, sdi)
                    sdi += 1

            @blk.tensor
            def _(t):
                for h in range(HC):
                    sm.wait(t, "gwl%d" % (h % 2), 16 * 3 * (h // 2 + 1))
                    base = (h % 2) * 3 * D
                    for bi, bat in enumerate(WBAT):
                        gidx = h * NWB + bi
                        if gidx >= 2:
                            sm.wait(t, "twev", gidx - 1)
                        pb = pwb[gidx % 2]
                        last = None
                        for k, c in enumerate(bat):
                            last = t.transpose(pb[:, ts(k, 128)],
                                               wnat[:, base + c * 128:
                                                    base + (c + 1) * 128], ident)
                        sm.tag(last, "twtr")
                    sm.wait(t, "twev", (h + 1) * NWB)
                    for mi in range(2):
                        woff = (1 - mi) * D
                        for mt in range(NMT):
                            idx = h * 2 * NMT + mi * NMT + mt
                            if idx >= 2:
                                sm.wait(t, "skv", idx - 1)
                            pb = kpb[idx % 2]
                            last = None
                            for c in range(ND):
                                last = t.matmul(pb,
                                                wT[:, woff + c * 128:
                                                   woff + (c + 1) * 128],
                                                xn_fm[:, c * S + mt * 512:
                                                      c * S + mt * 512 + 512],
                                                start=(c == 0), stop=(c == ND - 1))
                            sm.tag(last, "tkq")
                    for vg in range(NVG):
                        idx = h * NVG + vg
                        if idx >= 2:
                            sm.wait(t, "svv", idx - 1)
                        pb = vpb[idx % 2]
                        last = None
                        for u in range(4):
                            msub = vg * 4 + u
                            for c in range(ND):
                                last = t.matmul(pb[:, ts(u, 128)],
                                                xn_fm[:, c * S + msub * 128:
                                                      c * S + (msub + 1) * 128],
                                                wT[:, 2 * D + c * 128:
                                                   2 * D + (c + 1) * 128],
                                                start=(c == 0), stop=(c == ND - 1))
                        sm.tag(last, "tvv")

            @blk.scalar
            def _(s):
                for h in range(HC):
                    sm.wait(s, "sbself", h)
                    if h >= 1:
                        sm.wait(s, "tvv", h * NVG)
                    for bi, bat in enumerate(WBAT):
                        gidx = h * NWB + bi
                        sm.wait(s, "twtr", gidx + 1)
                        ncols = len(bat) * 128
                        sm.tag(s.copy(out=wT[:, bat[0] * 128: bat[0] * 128 + ncols],
                                      in_=pwb[gidx % 2][:, 0:ncols]), "twev")
                    for mi in range(2):
                        for mt in range(NMT):
                            idx = h * 2 * NMT + mi * NMT + mt
                            sm.wait(s, "tkq", idx + 1)
                            if idx >= 4:
                                sm.wait(s, "kqst%d" % (idx % 4),
                                        16 * ((idx - 4) // 4 + 1))
                            slot = kqb[:, ts(idx % 4, 512)]
                            if mi == 1:
                                ins = s.activation(out=slot, in_=kpb[idx % 2],
                                                   func=AF.Copy, scale=RSCALE)
                            else:
                                ins = s.copy(out=slot, in_=kpb[idx % 2])
                            sm.tag(ins, "skv")
                    for vg in range(NVG):
                        idx = h * NVG + vg
                        sm.wait(s, "tvv", idx + 1)
                        if idx >= 4:
                            sm.wait(s, "vst%d" % (idx % 4),
                                    16 * ((idx - 4) // 4 + 1))
                        sm.tag(s.copy(out=vb[:, ts(idx % 4, 512)],
                                      in_=vpb[idx % 2]), "svv")
                    sm.mark(s, "sbself")

            @blk.sync
            def _(sync):
                for h in range(HC):
                    for mi in range(2):
                        dsc = (k_sc, q_sc)[mi]
                        for mt in range(NMT):
                            idx = h * 2 * NMT + mi * NMT + mt
                            sm.wait(sync, "skv", idx + 1)
                            sm.tag(sync.dma_start(out=dsc[h][:, ts(mt, 512)],
                                                  in_=kqb[:, ts(idx % 4, 512)]),
                                   "kqst%d" % (idx % 4), 16)
                    for vg in range(NVG):
                        idx = h * NVG + vg
                        sm.wait(sync, "svv", idx + 1)
                        sm.tag(sync.dma_start(out=v_sc[h][:, ts(vg, 512)],
                                              in_=vb[:, ts(idx % 4, 512)]),
                               "vst%d" % (idx % 4), 16)
                for q in range(4):
                    sync.wait_ge(sm.get("kqst%d" % q), sm.val("kqst%d" % q))
                    sync.wait_ge(sm.get("vst%d" % q), sm.val("vst%d" % q))

            @blk.vector
            def _(v):
                sui = sm.n.get("sui_done", 0)
                k = 0
                while sui < NSTAT:
                    sm.wait(v, "v1bself", k)
                    stats_unit(v, sui)
                    sm.mark(v, "v1bself")
                    sui += 1
                    k += 1

    if cfg.stop_after == "b1b":
        return nc, es
    with (
        nc.psum_tensor("psb", [128, 3], F32) as psb_g,
        nc.named_scope("b1c"), nc.Block("b1c") as blk,
    ):
        psb = psb_g[:]

        @blk.vector
        def _(v):
            sm.tag(v.tensor_reduce(out=redc[:, 0:1], in_=s1cols[:, 0:NST1],
                                   axis=AX.X, op=ALU.add), "vqred")
            sm.tag(v.tensor_reduce(out=redc[:, 1:2], in_=s3cols[:, 0:NST1],
                                   axis=AX.X, op=ALU.add), "vqred")
            sm.tag(v.tensor_reduce(out=redc[:, 2:3], in_=s2cols[:, 0:NST1],
                                   axis=AX.X, op=ALU.add), "vqred")

        @blk.tensor
        def _(t):
            sm.wait(t, "vqred", 3)
            sm.tag(t.matmul(psb, ones_f32, redc[:, 0:3], start=True, stop=True),
                   "tred")

        @blk.scalar
        def _(s):
            sm.wait(s, "tred", 1)
            s.mul(out=h1c, in_=psb[:, 0:1], mul=0.5 / NW)
            s.mul(out=h3c, in_=psb[:, 1:2], mul=0.5 / NW)
            s.mul(out=h2c, in_=psb[:, 2:3], mul=0.5 / NW)
            s.mul(out=h1n, in_=psb[:, 0:1], mul=-0.5 / NW)
            s.mul(out=h3n, in_=psb[:, 1:2], mul=-0.5 / NW)
            s.mul(out=h2n, in_=psb[:, 2:3], mul=-0.5 / NW)

    if cfg.stop_after == "b1c":
        return nc, es
    pair_list = []
    for h in range(HC):
        for ti in range(NMT):
            nmc = 4 * ti + 4
            for pr in range(nmc // 2):
                pair_list.append((h, ti, pr, nmc))
    diag_before = []
    dcnt = 0
    for (h, ti, pr, nmc) in pair_list:
        diag_before.append(dcnt)
        if 2 * pr + 1 >= nmc - 4:
            dcnt += 1
    paircum = []
    acc = 0
    for ti in range(NMT):
        acc += (4 * ti + 4) // 2
        paircum.append(acc)

    def pair_cum(h, ti):
        return h * paircum[-1] + paircum[ti]

    with (
        nc.sbuf_tensor("ksb", [128, 2 * S], BF16) as ksb_g,
        nc.sbuf_tensor("qsb", [128, 2 * S], BF16) as qsb_g,
        nc.sbuf_tensor("vaug", [128, 2 * NCK * 129], BF16) as vaug_g,
        nc.sbuf_tensor("p_sb", [128, 2 * 1024], BF16) as p_sb_g,
        nc.sbuf_tensor("ztm", [128, 2 * S], BF16) as ztm_g,
        nc.sbuf_tensor("zfm", [128, 2 * S], BF16, is_transpose_dest=True) as zfm_g,
        nc.sbuf_tensor("recipb", [128, 8], F32) as recip_g,
        nc.psum_tensor("sp0", [128, 1024], F32) as sp0_g,
        nc.psum_tensor("sp1", [128, 1024], F32) as sp1_g,
        nc.psum_tensor("zp0", [128, 512], F32) as zp0_g,
        nc.psum_tensor("zp1", [128, 512], F32) as zp1_g,
        nc.psum_tensor("zp2", [128, 512], F32) as zp2_g,
        nc.psum_tensor("zp3", [128, 512], F32) as zp3_g,
        nc.named_scope("b2"), nc.Block("b2") as blk,
    ):
        ksb, qsb, vaug = ksb_g[:], qsb_g[:], vaug_g[:]
        p_sb, ztm, zfm, recipb = p_sb_g[:], ztm_g[:], zfm_g[:], recip_g[:]
        spb = [sp0_g[:], sp1_g[:]]
        zpb = [zp0_g[:], zp1_g[:], zp2_g[:], zp3_g[:]]

        def vaug_view(slot):
            return vaug[:, slot * NCK * 129:(slot + 1) * NCK * 129].rearrange(
                "p (m r) -> p m r", r=129)

        @blk.sync
        def _(sync):
            for h in range(HC + 1):
                if h < HC:
                    if h >= 2:
                        sm.wait(sync, "pav", pair_cum(h - 2, NMT - 1))
                    sm.tag(sync.dma_start(out=ksb[:, ts(h % 2, S)], in_=k_sc[h]),
                           "akl%d" % (h % 2), 16)
                    sm.tag(sync.dma_start(out=qsb[:, ts(h % 2, S)], in_=q_sc[h]),
                           "akl%d" % (h % 2), 16)
                    sm.tag(sync.dma_start(out=vaug_view(h % 2)[:, :, 0:128],
                                          in_=v_sc[h].rearrange(
                                              "p (m c) -> p m c", c=128)),
                           "akl%d" % (h % 2), 16)
                if h >= 1:
                    hz = h - 1
                    sm.wait(sync, "vdiv", (hz + 1) * NMT * 4)
                    for ci in range(NCK):
                        sm.tag(sync.dma_start_transpose(
                            out=zfm[:, (hz % 2) * S + ci * 128:
                                    (hz % 2) * S + (ci + 1) * 128],
                            in_=ztm[:, (hz % 2) * S + ci * 128:
                                    (hz % 2) * S + (ci + 1) * 128]), "ztr", 16)
                    sm.wait(sync, "ztr", sm.val("ztr"))
                    sm.tag(sync.dma_start(out=ag_in[hz],
                                          in_=zfm[:, ts(hz % 2, S)]),
                           "zst%d" % (hz % 2), 16)
            sync.wait_ge(sm.get("zst0"), sm.val("zst0"))
            sync.wait_ge(sm.get("zst1"), sm.val("zst1"))

        @blk.vector
        def _(v):
            v.memset(vaug_view(0)[:, :, 128:129], 1.0)
            sm.tag(v.memset(vaug_view(1)[:, :, 128:129], 1.0), "vone")
            for pidx, (h, ti, pr, nmc) in enumerate(pair_list):
                diag = (2 * pr + 1 >= nmc - 4)
                if diag:
                    sm.wait(v, "tsc", pidx + 1)
                    last = None
                    for u in range(2):
                        mc = 2 * pr + u
                        dd = mc - (nmc - 4)
                        if dd >= 0:
                            dlt = dd * 128
                            last = v.tensor_tensor(
                                out=spb[pidx % 2][:, u * 512 + dlt:
                                                  u * 512 + dlt + 128],
                                in0=spb[pidx % 2][:, u * 512 + dlt:
                                                  u * 512 + dlt + 128],
                                in1=maskst, op=ALU.add)
                    sm.tag(last, "vmk")
                if pr == nmc // 2 - 1:
                    tidx = h * NMT + ti
                    h2 = h % 2
                    sm.wait(v, "pav", pair_cum(h, ti))
                    if h >= 2:
                        sm.wait(v, "zst%d" % (h % 2), 16 * ((h - 2) // 2 + 1))
                    for tc in range(4):
                        sums = bass.AP(zpb[tc].tensor, zpb[tc].offset + 128,
                                       [list(zpb[tc].ap[0]), [1, 1]])
                        sm.tag(v.reciprocal(out=recipb[:, h2 * 4 + tc:
                                                       h2 * 4 + tc + 1],
                                            in_=sums), "vrc")
                    sm.wait(v, "vrc", sm.val("vrc"))
                    for tc in range(4):
                        ci = ti * 4 + tc
                        sm.tag(v.tensor_scalar(
                            out=ztm[:, h2 * S + ci * 128: h2 * S + (ci + 1) * 128],
                            in0=zpb[tc][:, 0:128],
                            scalar1=recipb[:, h2 * 4 + tc: h2 * 4 + tc + 1],
                            scalar2=None, op0=ALU.mult), "vdiv")

        @blk.scalar
        def _(s):
            for pidx, (h, ti, pr, nmc) in enumerate(pair_list):
                diag = (2 * pr + 1 >= nmc - 4)
                if diag:
                    sm.wait(s, "vmk", diag_before[pidx] + 1)
                else:
                    sm.wait(s, "tsc", pidx + 1)
                if pidx >= 2:
                    sm.wait(s, "pav", pidx - 1)
                full0 = (2 * pr) < (nmc - 4)
                full1 = (2 * pr + 1) < (nmc - 4)
                if full0 and full1:
                    sm.tag(s.activation(out=p_sb[:, ts(pidx % 2, 1024)],
                                        in_=spb[pidx % 2][:, 0:1024],
                                        func=AF.Exp), "sexp")
                else:
                    last = None
                    for u in range(2):
                        mc = 2 * pr + u
                        dd = mc - (nmc - 4)
                        dlt = max(dd, 0) * 128
                        last = s.activation(
                            out=p_sb[:, (pidx % 2) * 1024 + u * 512 + dlt:
                                     (pidx % 2) * 1024 + (u + 1) * 512],
                            in_=spb[pidx % 2][:, u * 512 + dlt:(u + 1) * 512],
                            func=AF.Exp)
                    sm.tag(last, "sexp")

        @blk.tensor
        def _(t):
            def av(pidx):
                h, ti, pr, nmc = pair_list[pidx]
                h2 = h % 2
                if pr == 0:
                    tidx = h * NMT + ti
                    if tidx >= 1:
                        sm.wait(t, "vdiv", tidx * 4)
                sm.wait(t, "sexp", pidx + 1)
                last = None
                for u in range(2):
                    mc = 2 * pr + u
                    dd = mc - (nmc - 4)
                    dlt = max(dd, 0) * 128
                    for tc in range(dlt // 128, 4):
                        last = t.matmul(
                            zpb[tc][:, 0:129],
                            p_sb[:, (pidx % 2) * 1024 + u * 512 + tc * 128:
                                 (pidx % 2) * 1024 + u * 512 + (tc + 1) * 128],
                            vaug_view(h2)[:, mc, :],
                            start=(mc == 0), stop=(mc == nmc - 4 + tc))
                sm.tag(last, "pav")

            for pidx, (h, ti, pr, nmc) in enumerate(pair_list):
                h2 = h % 2
                if pr == 0:
                    if ti == 0:
                        sm.wait(t, "akl%d" % (h % 2), 16 * 3 * (h // 2 + 1))
                        if h == 0:
                            sm.wait(t, "vone", 1)
                if pidx >= 2:
                    sm.wait(t, "sexp", pidx - 1)
                last = None
                for u in range(2):
                    mc = 2 * pr + u
                    dd = mc - (nmc - 4)
                    dlt = max(dd, 0) * 128
                    last = t.matmul(
                        spb[pidx % 2][:, u * 512 + dlt:(u + 1) * 512],
                        ksb[:, h2 * S + mc * 128: h2 * S + (mc + 1) * 128],
                        qsb[:, h2 * S + ti * 512 + dlt: h2 * S + (ti + 1) * 512],
                        start=True, stop=True)
                sm.tag(last, "tsc")
                if pr >= 1:
                    av(pidx - 1)
                if pr == nmc // 2 - 1:
                    av(pidx)

        @blk.gpsimd
        def _(g):
            sm.wait(g, "zst0", sm.val("zst0"))
            sm.wait(g, "zst1", sm.val("zst1"))
            cc = g.collective_compute(
                "AllGather", ALU.bypass,
                replica_groups=[[0, 1], [2, 3], [4, 5], [6, 7]],
                ins=[ag_in], outs=[ag_out])
            sm.tag(cc, "ccd")

    if cfg.stop_after == "b2":
        return nc, es
    with (
        nc.sbuf_tensor("w2b", [128, 2 * J], F32) as w2b_g,
        nc.sbuf_tensor("sg2a", [128, J], BF16) as sg2a_g,
        nc.sbuf_tensor("sg2b", [128, J], BF16) as sg2b_g,
        nc.sbuf_tensor("r2tm", [128, 2 * J], BF16) as r2tm_g,
        nc.sbuf_tensor("r2fmb", [128, 2 * NJ * 128], BF16) as r2fmb_g,
        nc.psum_tensor("pq0", [128, 1024], BF16) as pq0_g,
        nc.psum_tensor("pq1", [128, 1024], BF16) as pq1_g,
        nc.named_scope("b4a"), nc.Block("b4a") as blk,
    ):
        w2b, sg2a, sg2b = w2b_g[:], sg2a_g[:], sg2b_g[:]
        r2tm, r2fmb = r2tm_g[:], r2fmb_g[:]
        pqb = [pq0_g[:], pq1_g[:]]
        QBAT = batches(NJ, 8)
        NQB = len(QBAT)

        @blk.sync
        def _(sync):
            for dc in range(ND):
                if dc >= 2:
                    sm.wait(sync, "sq2", dc - 1)
                sm.tag(sync.dma_start(out=w2b[:, ts(dc % 2, J)],
                                      in_=w2[ts(dc, 128), :]), "w2l%d" % (dc % 2), 16)
                if dc >= 1:
                    sm.wait(sync, "evq", dc * NQB)
                    sm.tag(sync.dma_start(out=r2_sc[dc - 1],
                                          in_=r2fmb[:, ts((dc - 1) % 2, NJ * 128)]),
                           "r2st%d" % ((dc - 1) % 2), 16)
            sm.wait(sync, "evq", ND * NQB)
            sm.tag(sync.dma_start(out=r2_sc[ND - 1],
                                  in_=r2fmb[:, ts((ND - 1) % 2, NJ * 128)]),
                   "r2st%d" % ((ND - 1) % 2), 16)
            sync.wait_ge(sm.get("r2st0"), sm.val("r2st0"))
            sync.wait_ge(sm.get("r2st1"), sm.val("r2st1"))

        @blk.scalar
        def _(s):
            for dc in range(ND):
                sm.wait(s, "sqself", dc)
                sm.wait(s, "w2l%d" % (dc % 2), 16 * (dc // 2 + 1))
                sm.wait(s, "vq2", dc)
                s.activation(out=sg2a, in_=w2b[:, ts(dc % 2, J)], func=AF.Sign,
                             bias=h2n)
                sm.tag(s.activation(out=sg2b, in_=w2b[:, ts(dc % 2, J)],
                                    func=AF.Sign, bias=h2c), "sq2")
                sm.mark(s, "sqself")

        @blk.vector
        def _(v):
            v.memset(r2fmb[:], 0.0)

            def evq(dc):
                for bi, bat in enumerate(QBAT):
                    gq = dc * NQB + bi
                    sm.wait(v, "tq2", gq + 1)
                    if dc >= 2:
                        sm.wait(v, "r2st%d" % (dc % 2), 16 * ((dc - 2) // 2 + 1))
                    lo = bat[0] * 128
                    base = (dc % 2) * NJ * 128
                    if bat[-1] == NJ - 1:
                        hi = bat[-1] * 128
                        if hi > lo:
                            v.tensor_copy(out=r2fmb[:, base + lo: base + hi],
                                          in_=pqb[gq % 2][:, 0:hi - lo])
                        sm.tag(v.tensor_copy(
                            out=r2fmb[0:JLAST, base + hi: base + hi + 128],
                            in_=pqb[gq % 2][0:JLAST, hi - lo: hi - lo + 128]),
                            "evq")
                    else:
                        hi = bat[-1] * 128 + 128
                        sm.tag(v.tensor_copy(out=r2fmb[:, base + lo: base + hi],
                                             in_=pqb[gq % 2][:, 0:hi - lo]), "evq")

            for dc in range(ND):
                sm.wait(v, "vqself", dc)
                sm.wait(v, "sq2", dc + 1)
                if dc >= 2:
                    sm.wait(v, "tq2", (dc - 1) * NQB)
                sm.tag(v.tensor_add(out=r2tm[:, ts(dc % 2, J)], in0=sg2a,
                                    in1=sg2b), "vq2")
                if dc >= 1:
                    evq(dc - 1)
                sm.mark(v, "vqself")
            evq(ND - 1)

        @blk.tensor
        def _(t):
            for dc in range(ND):
                sm.wait(t, "vq2", dc + 1)
                for bi, bat in enumerate(QBAT):
                    gidx = dc * NQB + bi
                    if gidx >= 2:
                        sm.wait(t, "evq", gidx - 1)
                    pb = pqb[gidx % 2]
                    last = None
                    for k, jc in enumerate(bat):
                        rows = 128 if jc < NJ - 1 else JLAST
                        last = t.transpose(pb[0:rows, ts(k, 128)],
                                           r2tm[:, (dc % 2) * J + jc * 128:
                                                (dc % 2) * J + jc * 128 + rows],
                                           ident)
                    sm.tag(last, "tq2")

    if cfg.stop_after == "b4a":
        return nc, es
    with (
        nc.sbuf_tensor("xnow", [128, ND * TOWN], BF16) as xnow_g,
        nc.sbuf_tensor("w13b", [128, 4 * D], F32) as w13b_g,
        nc.sbuf_tensor("sg1a", [128, D], BF16) as sg1a_g,
        nc.sbuf_tensor("sg1b", [128, D], BF16) as sg1b_g,
        nc.sbuf_tensor("r13tm", [128, 4 * D], BF16) as r13tm_g,
        nc.sbuf_tensor("r13fm", [128, 2 * 2 * D], BF16) as r13fm_g,
        nc.sbuf_tensor("shb", [128, 2 * 512], F32) as shb_g,
        nc.sbuf_tensor("gb", [128, 2 * TOWN], BF16) as gb_g,
        nc.psum_tensor("pf0", [128, 1024], BF16) as pf0_g,
        nc.psum_tensor("pf1", [128, 1024], BF16) as pf1_g,
        nc.psum_tensor("h1p0", [128, 512], F32) as h1p0_g,
        nc.psum_tensor("h3p0", [128, 512], F32) as h3p0_g,
        nc.psum_tensor("h1p1", [128, 512], F32) as h1p1_g,
        nc.psum_tensor("h3p1", [128, 512], F32) as h3p1_g,
        nc.named_scope("b4b"), nc.Block("b4b") as blk,
    ):
        xnow, w13b = xnow_g[:], w13b_g[:]
        sg1a, sg1b, r13tm, r13fm = sg1a_g[:], sg1b_g[:], r13tm_g[:], r13fm_g[:]
        shb, gb = shb_g[:], gb_g[:]
        pfb = [pf0_g[:], pf1_g[:]]
        h1p = [h1p0_g[:], h1p1_g[:]]
        h3p = [h3p0_g[:], h3p1_g[:]]
        FBAT = batches(2 * ND, 8)
        NFB = len(FBAT)

        @blk.sync
        def _(sync):
            for c in range(ND):
                sm.tag(sync.dma_start(out=xnow[:, ts(c, TOWN)], in_=xn_own_sc[c]),
                       "xol", 16)
            for jc in range(NJ):
                rows = 128 if jc < NJ - 1 else JLAST
                if jc >= 2:
                    sm.wait(sync, "sgn1", 2 * (jc - 1))
                sm.tag(sync.dma_start(
                    out=w13b[0:rows, (jc % 2) * 2 * D:(jc % 2) * 2 * D + D],
                    in_=w1[jc * 128: jc * 128 + rows, :]), "w13l%d" % (jc % 2), 16)
                sm.tag(sync.dma_start(
                    out=w13b[0:rows, (jc % 2) * 2 * D + D:(jc % 2) * 2 * D + 2 * D],
                    in_=w3[jc * 128: jc * 128 + rows, :]), "w13l%d" % (jc % 2), 16)
                if jc >= 1:
                    pj = jc - 1
                    sm.wait(sync, "gev", (pj + 1) * NTT)
                    sm.tag(sync.dma_start(out=g_sc[pj],
                                          in_=gb[:, ts(pj % 2, TOWN)]),
                           "gst%d" % (pj % 2), 16)
            pj = NJ - 1
            sm.wait(sync, "gev", (pj + 1) * NTT)
            sm.tag(sync.dma_start(out=g_sc[pj],
                                  in_=gb[:, ts(pj % 2, TOWN)]),
                   "gst%d" % (pj % 2), 16)
            sync.wait_ge(sm.get("gst0"), sm.val("gst0"))
            sync.wait_ge(sm.get("gst1"), sm.val("gst1"))

        @blk.scalar
        def _(s):
            for jc in range(NJ):
                rows = 128 if jc < NJ - 1 else JLAST
                sm.wait(s, "sfself", jc)
                sm.wait(s, "w13l%d" % (jc % 2), 32 * (jc // 2 + 1))
                sm.wait(s, "vr1", 2 * jc)
                sl = (jc % 2) * 2 * D
                s.activation(out=sg1a[0:rows, :], in_=w13b[0:rows, sl:sl + D],
                             func=AF.Sign, bias=h1n[0:rows, :])
                sm.tag(s.activation(out=sg1b[0:rows, :],
                                    in_=w13b[0:rows, sl:sl + D],
                                    func=AF.Sign, bias=h1c[0:rows, :]), "sgn1")
                sm.wait(s, "vr1", 2 * jc + 1)
                s.activation(out=sg1a[0:rows, :],
                             in_=w13b[0:rows, sl + D:sl + 2 * D],
                             func=AF.Sign, bias=h3n[0:rows, :])
                sm.tag(s.activation(out=sg1b[0:rows, :],
                                    in_=w13b[0:rows, sl + D:sl + 2 * D],
                                    func=AF.Sign, bias=h3c[0:rows, :]), "sgn1")
                for tt in range(NTT):
                    idx = jc * NTT + tt
                    sm.wait(s, "th1", idx + 1)
                    if idx >= 2:
                        sm.wait(s, "gev", idx - 1)
                    sm.tag(s.activation(out=shb[0:rows, ts(tt % 2, 512)],
                                        in_=h1p[tt % 2][0:rows, :], func=AF.Silu,
                                        scale=h1c[0:rows, :]), "ssil")
                sm.mark(s, "sfself")

        @blk.vector
        def _(v):
            v.memset(gb[:], 0.0)

            def evf(jc):
                rows = 128 if jc < NJ - 1 else JLAST
                for bi, bat in enumerate(FBAT):
                    gf = jc * NFB + bi
                    sm.wait(v, "tr13", gf + 1)
                    if jc >= 2:
                        sm.wait(v, "th3", (jc - 1) * NTT)
                    base = (jc % 2) * 2 * D
                    if rows == 128:
                        lo, hi = bat[0] * 128, bat[-1] * 128 + 128
                        sm.tag(v.tensor_copy(out=r13fm[:, base + lo: base + hi],
                                             in_=pfb[gf % 2][:, 0:hi - lo]), "evf")
                    else:
                        last = None
                        for k, c in enumerate(bat):
                            last = v.tensor_copy(
                                out=r13fm[:, base + c * 128: base + c * 128 + rows],
                                in_=pfb[gf % 2][:, k * 128: k * 128 + rows])
                        sm.tag(last, "evf")

            for jc in range(NJ):
                rows = 128 if jc < NJ - 1 else JLAST
                sm.wait(v, "vfself", jc)
                if jc >= 2:
                    sm.wait(v, "gst%d" % (jc % 2), 16 * ((jc - 2) // 2 + 1))
                sl = (jc % 2) * 2 * D
                for half in range(2):
                    sm.wait(v, "sgn1", 2 * jc + half + 1)
                    if jc >= 1:
                        sm.wait(v, "tr13", jc * NFB)
                    sm.tag(v.tensor_add(out=r13tm[0:rows, sl + half * D:
                                                  sl + (half + 1) * D],
                                        in0=sg1a[0:rows, :], in1=sg1b[0:rows, :]),
                           "vr1")
                evf(jc)
                for tt in range(NTT):
                    idx = jc * NTT + tt
                    sm.wait(v, "ssil", idx + 1)
                    sm.wait(v, "th3", idx + 1)
                    sm.tag(v.scalar_tensor_tensor(
                        out=gb[0:rows, (jc % 2) * TOWN + tt * 512:
                               (jc % 2) * TOWN + (tt + 1) * 512],
                        in0=h3p[tt % 2][0:rows, :], scalar=h3c[0:rows, :],
                        in1=shb[0:rows, ts(tt % 2, 512)],
                        op0=ALU.mult, op1=ALU.mult), "gev")
                sm.mark(v, "vfself")

        @blk.tensor
        def _(t):
            sm.wait(t, "xol", 16 * ND)
            for jc in range(NJ):
                rows = 128 if jc < NJ - 1 else JLAST
                sm.wait(t, "vr1", 2 * jc + 2)
                for bi, bat in enumerate(FBAT):
                    gidx = jc * NFB + bi
                    if gidx >= 2:
                        sm.wait(t, "evf", gidx - 1)
                    pb = pfb[gidx % 2]
                    last = None
                    for k, c in enumerate(bat):
                        last = t.transpose(pb[0:128, k * 128: k * 128 + rows],
                                           r13tm[0:rows,
                                                 (jc % 2) * 2 * D + c * 128:
                                                 (jc % 2) * 2 * D + (c + 1) * 128],
                                           ident[0:rows, 0:rows])
                    sm.tag(last, "tr13")
                sm.wait(t, "evf", jc * NFB + NFB // 2)
                for tt in range(NTT):
                    idx = jc * NTT + tt
                    if idx >= 2:
                        sm.wait(t, "ssil", idx - 1)
                        sm.wait(t, "gev", idx - 1)
                    last = None
                    for c in range(ND):
                        last = t.matmul(h1p[tt % 2][0:rows, :],
                                        r13fm[:, (jc % 2) * 2 * D + c * 128:
                                              (jc % 2) * 2 * D + c * 128 + rows],
                                        xnow[:, c * TOWN + tt * 512:
                                             c * TOWN + tt * 512 + 512],
                                        start=(c == 0), stop=(c == ND - 1))
                    sm.tag(last, "th1")
                    if tt == 0:
                        sm.wait(t, "evf", (jc + 1) * NFB)
                    last = None
                    for c in range(ND):
                        last = t.matmul(h3p[tt % 2][0:rows, :],
                                        r13fm[:, (jc % 2) * 2 * D + D + c * 128:
                                              (jc % 2) * 2 * D + D + c * 128 + rows],
                                        xnow[:, c * TOWN + tt * 512:
                                             c * TOWN + tt * 512 + 512],
                                        start=(c == 0), stop=(c == ND - 1))
                    sm.tag(last, "th3")

    if cfg.stop_after == "b4b":
        return nc, es
    with (
        nc.sbuf_tensor("wonat", [128, 2 * ND * 128], BF16) as wonat_g,
        nc.sbuf_tensor("wot", [128, NH * D], BF16) as wot_g,
        nc.sbuf_tensor("z16", [128, 2 * NH * 128], BF16) as z16_g,
        nc.sbuf_tensor("attnb", [128, 4 * 512], F32) as attnb_g,
        nc.psum_tensor("po0", [128, 1024], BF16) as po0_g,
        nc.psum_tensor("po1", [128, 1024], BF16) as po1_g,
        nc.psum_tensor("ap0", [128, 512], F32) as ap0_g,
        nc.psum_tensor("ap1", [128, 512], F32) as ap1_g,
        nc.psum_tensor("ap2", [128, 512], F32) as ap2_g,
        nc.psum_tensor("ap3", [128, 512], F32) as ap3_g,
        nc.named_scope("b3"), nc.Block("b3") as blk,
    ):
        wonat, wot, z16, attnb = wonat_g[:], wot_g[:], z16_g[:], attnb_g[:]
        pob = [po0_g[:], po1_g[:]]
        apb = [ap0_g[:], ap1_g[:], ap2_g[:], ap3_g[:]]
        OBAT = batches(ND, 8)
        NOB = len(OBAT)

        @blk.gpsimd
        def _(g):
            for gh in range(NH):
                if gh >= 2:
                    sm.wait(g, "wotr", (gh - 1) * NOB)
                sm.tag(g.dma_start(
                    out=wonat[:, (gh % 2) * ND * 128:(gh % 2 + 1) * ND * 128]
                        .rearrange("p (c e) -> p c e", c=ND),
                    in_=wo[gh].rearrange("(c p) e -> p c e", p=128)),
                    "wol%d" % (gh % 2), 16)
            sm.wait(g, "ccd", 1)
            ovs = [g.value_load(soff[0:1, tc:tc + 1]) for tc in range(NTC)]
            for tc in range(NTC):
                if tc >= 2:
                    sm.wait(g, "tao", (tc - 1) * NDS)
                src = ag_out[:, :, bass.ds(ovs[tc], 128)].rearrange("g p c -> p g c")
                sm.tag(g.dma_start(
                    out=z16[:, (tc % 2) * NH * 128:(tc % 2 + 1) * NH * 128]
                        .rearrange("p (g c) -> p g c", g=NH),
                    in_=src), "z16l%d" % (tc % 2), 16)

        @blk.tensor
        def _(t):
            for gh in range(NH):
                sm.wait(t, "wol%d" % (gh % 2), 16 * (gh // 2 + 1))
                for bi, bat in enumerate(OBAT):
                    gidx = gh * NOB + bi
                    if gidx >= 2:
                        sm.wait(t, "woev", gidx - 1)
                    pb = pob[gidx % 2]
                    last = None
                    for k, c in enumerate(bat):
                        last = t.transpose(
                            pb[:, ts(k, 128)],
                            wonat[:, (gh % 2) * ND * 128 + c * 128:
                                  (gh % 2) * ND * 128 + (c + 1) * 128],
                            ident)
                    sm.tag(last, "wotr")
            sm.wait(t, "woev", NH * NOB)
            for tc in range(NTC):
                sm.wait(t, "z16l%d" % (tc % 2), 16 * (tc // 2 + 1))
                for ds_ in range(NDS):
                    idx = tc * NDS + ds_
                    if idx >= 4:
                        sm.wait(t, "aoev", idx - 3)
                    pb = apb[idx % 4]
                    last = None
                    for gh in range(NH):
                        last = t.matmul(pb,
                                        z16[:, (tc % 2) * NH * 128 + gh * 128:
                                            (tc % 2) * NH * 128 + (gh + 1) * 128],
                                        wot[:, gh * D + ds_ * 512:
                                            gh * D + ds_ * 512 + 512],
                                        start=(gh == 0), stop=(gh == NH - 1))
                    sm.tag(last, "tao")

        @blk.scalar
        def _(s):
            for gh in range(NH):
                for bi, bat in enumerate(OBAT):
                    gidx = gh * NOB + bi
                    sm.wait(s, "wotr", gidx + 1)
                    ncols = len(bat) * 128
                    sm.tag(s.copy(out=wot[:, gh * D + bat[0] * 128:
                                          gh * D + bat[0] * 128 + ncols],
                                  in_=pob[gidx % 2][:, 0:ncols]), "woev")

        @blk.vector
        def _(v):
            for tc in range(NTC):
                for ds_ in range(NDS):
                    idx = tc * NDS + ds_
                    sm.wait(v, "tao", idx + 1)
                    if idx >= 4:
                        sm.wait(v, "aost%d" % (idx % 4), 16 * ((idx - 4) // 4 + 1))
                    sm.tag(v.tensor_copy(out=attnb[:, ts(idx % 4, 512)],
                                         in_=apb[idx % 4]), "aoev")

        @blk.sync
        def _(sync):
            for tc in range(NTC):
                for ds_ in range(NDS):
                    idx = tc * NDS + ds_
                    sm.wait(sync, "aoev", idx + 1)
                    sm.tag(sync.dma_start(out=attn_sc[ts(tc, 128), ts(ds_, 512)],
                                          in_=attnb[:, ts(idx % 4, 512)]),
                           "aost%d" % (idx % 4), 16)
            for q in range(4):
                sync.wait_ge(sm.get("aost%d" % q), sm.val("aost%d" % q))

    if cfg.stop_after == "b3":
        return nc, es
    NTG = TOWN // 256
    with (
        nc.sbuf_tensor("r2sb", [128, NDH * NJ * 128], BF16) as r2sb_g,
        nc.sbuf_tensor("g2sb", [128, 2 * NJ * 256], BF16) as g2sb_g,
        nc.sbuf_tensor("axb", [128, 4 * NDH * 128], F32) as axb_g,
        nc.sbuf_tensor("outb", [128, 2 * NDH * 128], F32) as outb_g,
        nc.psum_tensor("fo0", [128, 512], F32) as fo0_g,
        nc.psum_tensor("fo1", [128, 512], F32) as fo1_g,
        nc.psum_tensor("fo2", [128, 512], F32) as fo2_g,
        nc.psum_tensor("fo3", [128, 512], F32) as fo3_g,
        nc.psum_tensor("fo4", [128, 512], F32) as fo4_g,
        nc.psum_tensor("fo5", [128, 512], F32) as fo5_g,
        nc.psum_tensor("fo6", [128, 512], F32) as fo6_g,
        nc.psum_tensor("fo7", [128, 512], F32) as fo7_g,
        nc.named_scope("b5"), nc.Block("b5") as blk,
    ):
        r2sb, g2sb, axb, outb = r2sb_g[:], g2sb_g[:], axb_g[:], outb_g[:]
        fob = [fo0_g[:], fo1_g[:], fo2_g[:], fo3_g[:],
               fo4_g[:], fo5_g[:], fo6_g[:], fo7_g[:]]
        HW = NDH * 128

        @blk.sync
        def _(sync):
            for dh in range(2):
                if dh == 1:
                    sm.wait(sync, "tfo", NTG)
                for k in range(NDH):
                    if "r2" in cfg.b5_skip:
                        break
                    sm.tag(sync.dma_start(out=r2sb[:, ts(k, NJ * 128)],
                                          in_=r2_sc[dh * NDH + k]), "r2l", 16)
                for tg in range(NTG):
                    gidx = dh * NTG + tg
                    if gidx >= 2:
                        sm.wait(sync, "tfo", gidx - 1)
                    for bi, jb in enumerate(batches(NJ, 6)):
                        if "g2" in cfg.b5_skip:
                            break
                        j0, jn = jb[0], len(jb)
                        src = g_sc[j0:j0 + jn, :, ts(tg, 256)].rearrange(
                            "j p c -> p j c")
                        half = 0 if bi < 4 else 1
                        sm.tag(sync.dma_start(
                            out=g2sb[:, (gidx % 2) * NJ * 256 + j0 * 256:
                                     (gidx % 2) * NJ * 256 + (j0 + jn) * 256]
                                .rearrange("p (j c) -> p j c", j=jn),
                            in_=src), "g2l%d_%d" % (gidx % 2, half), 16)
                    for u in range(2):
                        tc = 2 * tg + u
                        gi = dh * NTC + tc
                        if gi >= 2:
                            sm.wait(sync, "foev", (gi - 1) * NDSS)
                        if "ax" not in cfg.b5_skip:
                            sm.tag(sync.dma_start(
                                out=axb[:, (gi % 2) * 2 * HW:
                                        (gi % 2) * 2 * HW + HW],
                                in_=attn_sc[ts(tc, 128), dh * HW:(dh + 1) * HW]),
                                "axl%d" % (gi % 2), 16)
                            sm.tag(sync.dma_start(
                                out=axb[:, (gi % 2) * 2 * HW + HW:
                                        (gi % 2 + 1) * 2 * HW],
                                in_=xo[ts(tc, 128), dh * HW:(dh + 1) * HW]),
                                "axl%d" % (gi % 2), 16)
                        if gi >= 1:
                            pgi = gi - 1
                            sm.wait(sync, "foev", (pgi + 1) * NDSS)
                            sm.tag(sync.dma_start(
                                out=out[ts(pgi % NTC, 128),
                                        (pgi // NTC) * HW:(pgi // NTC + 1) * HW],
                                in_=outb[:, ts(pgi % 2, HW)]),
                                "fst%d" % (pgi % 2), 16)
            pgi = 2 * NTC - 1
            sm.wait(sync, "foev", (pgi + 1) * NDSS)
            sm.tag(sync.dma_start(
                out=out[ts(pgi % NTC, 128),
                        (pgi // NTC) * HW:(pgi // NTC + 1) * HW],
                in_=outb[:, ts(pgi % 2, HW)]), "fst%d" % (pgi % 2), 16)
            sync.wait_ge(sm.get("fst0"), sm.val("fst0"))
            sync.wait_ge(sm.get("fst1"), sm.val("fst1"))

        @blk.tensor
        def _(t):
            r2v = r2sb.rearrange("p (k x) -> p k x", k=NDH)
            for dh in range(2):
                if "r2" not in cfg.b5_skip:
                    sm.wait(t, "r2l", 16 * NDH * (dh + 1))
                for tg in range(NTG):
                    gidx = dh * NTG + tg
                    nb_g2 = len(batches(NJ, 6))
                    nh0_g2 = min(nb_g2, 4)
                    nh1_g2 = nb_g2 - nh0_g2
                    grp = gidx % 2
                    bank0 = (tg % 2) * 2 * NDSS
                    if gidx >= 2:
                        sm.wait(t, "foev", (gidx - 1) * 2 * NDSS)
                    if "mm" in cfg.b5_skip:
                        sm.mark(t, "tfo")
                    else:
                        last = None
                        for jc in range(NJ):
                            rows = 128 if jc < NJ - 1 else JLAST
                            if "g2" not in cfg.b5_skip and jc in (0, 24):
                                half = 0 if jc == 0 else 1
                                nhb = nh0_g2 if half == 0 else nh1_g2
                                sm.wait(t, "g2l%d_%d" % (gidx % 2, half),
                                        16 * nhb * (gidx // 2 + 1))
                            for u in range(2):
                                for dss in range(NDSS):
                                    rhs = r2v[:, dss * 4:(dss + 1) * 4,
                                              jc * 128: jc * 128 + 128]
                                    last = t.matmul(
                                        fob[bank0 + u * NDSS + dss],
                                        g2sb[0:rows,
                                             grp * NJ * 256 + jc * 256 + u * 128:
                                             grp * NJ * 256 + jc * 256 + (u + 1) * 128],
                                        rhs[0:rows, :, :],
                                        start=(jc == 0), stop=(jc == NJ - 1))
                        sm.tag(last, "tfo")

        @blk.vector
        def _(v):
            for dh in range(2):
                for tg in range(NTG):
                    gidx = dh * NTG + tg
                    sm.wait(v, "tfo", gidx + 1)
                    bank0 = (tg % 2) * 2 * NDSS
                    for u in range(2):
                        gi = dh * NTC + tg * 2 + u
                        if "ax" not in cfg.b5_skip:
                            sm.wait(v, "axl%d" % (gi % 2), 32 * (gi // 2 + 1))
                        if gi >= 2:
                            sm.wait(v, "fst%d" % (gi % 2), 16 * ((gi - 2) // 2 + 1))
                        for dss in range(NDSS):
                            o = outb[:, (gi % 2) * HW + dss * 512:
                                     (gi % 2) * HW + (dss + 1) * 512]
                            if "vec" in cfg.b5_skip:
                                v.memset(o, 0.0)
                                sm.mark(v, "foev")
                                continue
                            v.scalar_tensor_tensor(
                                out=o, in0=fob[bank0 + u * NDSS + dss], scalar=h2c,
                                in1=axb[:, (gi % 2) * 2 * HW + dss * 512:
                                        (gi % 2) * 2 * HW + (dss + 1) * 512],
                                op0=ALU.mult, op1=ALU.add)
                            sm.tag(v.tensor_add(
                                out=o, in0=o,
                                in1=axb[:, (gi % 2) * 2 * HW + HW + dss * 512:
                                        (gi % 2) * 2 * HW + HW + (dss + 1) * 512]),
                                "foev")

    return nc, es


def kernel(L0_attn_x, L0_attn_W_Q, L0_attn_W_K, L0_attn_W_V, L0_attn_W_O,
           L0_ffn_w1, L0_ffn_w3, L0_ffn_w2):
    cfg = Cfg()
    x = np.ascontiguousarray(np.asarray(L0_attn_x, dtype=np.float32))
    WQ = np.ascontiguousarray(np.asarray(L0_attn_W_Q, dtype=np.float32))
    WK = np.ascontiguousarray(np.asarray(L0_attn_W_K, dtype=np.float32))
    WV = np.ascontiguousarray(np.asarray(L0_attn_W_V, dtype=np.float32))
    WO = np.ascontiguousarray(np.asarray(L0_attn_W_O, dtype=np.float32))
    w1 = np.ascontiguousarray(np.asarray(L0_ffn_w1, dtype=np.float32))
    w3 = np.ascontiguousarray(np.asarray(L0_ffn_w3, dtype=np.float32))
    w2 = np.ascontiguousarray(np.asarray(L0_ffn_w2, dtype=np.float32))

    nc, es = build(cfg)
    in_maps = []
    for c in range(8):
        b, hh = c // 2, c % 2
        offs = np.arange(cfg.NTC, dtype=np.int32) * 128 + hh * cfg.TOWN
        in_maps.append({
            "xk": x[b],
            "xo": np.ascontiguousarray(x[b, hh * cfg.TOWN:(hh + 1) * cfg.TOWN]),
            "wq": np.ascontiguousarray(WQ[hh * cfg.HC:(hh + 1) * cfg.HC]),
            "wk": np.ascontiguousarray(WK[hh * cfg.HC:(hh + 1) * cfg.HC]),
            "wv": np.ascontiguousarray(WV[hh * cfg.HC:(hh + 1) * cfg.HC]),
            "wo": WO,
            "w1": w1, "w3": w3, "w2": w2,
            "roff": offs.reshape(1, cfg.NTC),
        })
    res = run_bass_kernel_spmd(nc, in_maps, core_ids=list(range(8)))
    es.close()
    outp = np.empty((4, cfg.S, cfg.D), dtype=np.float32)
    for c in range(8):
        b, hh = c // 2, c % 2
        outp[b, hh * cfg.TOWN:(hh + 1) * cfg.TOWN] = res.results[c]["out"]
    return outp
